# version 11
# speedup vs baseline: 2.0955x; 1.0789x over previous
import math
from contextlib import ExitStack

import numpy as np
import ml_dtypes

import concourse.bass as bass
import concourse.mybir as mybir
from concourse.bass_utils import run_bass_kernel_spmd
from concourse.bass_core_rust import IndirectOffsetOnAxis

F32 = mybir.dt.float32
BF16 = mybir.dt.bfloat16
AF = mybir.ActivationFunctionType
ALU = mybir.AluOpType
AX = mybir.AxisListType

B = 2
S = 2048
D = 1024
H = 16
DH = 64
T = B * S
E = 8
FF = 2048
NQ = 4
NCORES = 8
BFD = ml_dtypes.bfloat16


class _SemW:
    __slots__ = ("h", "count")

    def __init__(self, h):
        self.h = h
        self.count = 0


class Prog:
    def __init__(self, nc):
        self.nc = nc
        self.steps = []
        self.stack = ExitStack()
        self.phases = []
        self._semi = 0
        self._wm = {}

    def open_phase(self):
        self.phases.append(ExitStack())

    def close_phase(self):
        self.phases.pop().close()

    def _alloc(self, guard):
        tgt = self.phases[-1] if self.phases else self.stack
        return tgt.enter_context(guard)

    def sem(self, name=None):
        h = self.stack.enter_context(self.nc.semaphore(name or f"s{self._semi}"))
        self._semi += 1
        return _SemW(h)

    def _uname(self, name):
        self._semi += 1
        return f"{name}_{self._semi}"

    def sbuf(self, name, shape, dtype):
        return self._alloc(self.nc.sbuf_tensor(self._uname(name), shape, dtype))

    def psum(self, name, shape, dtype=F32):
        return self._alloc(self.nc.psum_tensor(self._uname(name), shape, dtype))

    def step(self, eng, fn, waits=(), incs=()):
        w = []
        for (s, v) in waits:
            v = s.count if v is None else v
            key = (eng, id(s))
            if v > self._wm.get(key, 0):
                self._wm[key] = v
                w.append((s, v))
        for (s, n) in incs:
            s.count += n
        self.steps.append((eng, fn, w, list(incs)))

    def build_block(self):
        nc, steps = self.nc, self.steps

        def body(name):
            def run(eng):
                for (e, fn, waits, incs) in steps:
                    if e != name:
                        continue
                    for (s, v) in waits:
                        eng.wait_ge(s.h, v)
                    if fn is None:
                        continue
                    inst = fn(eng)
                    for (s, n) in incs:
                        inst = inst.then_inc(s.h, n)
            return run

        with nc.allow_low_precision(reason="bf16 activations, fp32 accumulation"):
            with nc.Block() as block:
                block.tensor(body("tensor"))
                block.vector(body("vector"))
                block.scalar(body("scalar"))
                block.sync(body("sync"))
                block.gpsimd(body("gpsimd"))


def declare_inputs(nc):
    d = {}
    d["x"] = nc.dram_tensor("x", [B, S, D], F32, kind="ExternalInput").ap()
    d["wqkv_r"] = nc.dram_tensor("wqkv_r", [24, 128, 8, 128], F32,
                                 kind="ExternalInput").ap()
    d["bqkv_pc"] = nc.dram_tensor("bqkv_pc", [128, 24], F32, kind="ExternalInput").ap()
    d["wout_r"] = nc.dram_tensor("wout_r", [128, 8, 1024], F32,
                                 kind="ExternalInput").ap()
    d["bout"] = nc.dram_tensor("bout", [D], F32, kind="ExternalInput").ap()
    d["wg_r"] = nc.dram_tensor("wg_r", [128, 8, 8], F32, kind="ExternalInput").ap()
    d["w1r"] = nc.dram_tensor("w1r", [E, 8, 128, 8, 256], F32,
                              kind="ExternalInput").ap()
    d["w2r"] = nc.dram_tensor("w2r", [E, 2, 128, 16, 512], F32,
                              kind="ExternalInput").ap()
    d["idn_bf"] = nc.dram_tensor("idn_bf", [128, 128], BF16, kind="ExternalInput").ap()
    d["idn_f32"] = nc.dram_tensor("idn_f32", [128, 128], F32, kind="ExternalInput").ap()
    d["ones_f32"] = nc.dram_tensor("ones_f32", [1, 128], F32, kind="ExternalInput").ap()
    d["smat"] = nc.dram_tensor("smat", [64, 8], F32, kind="ExternalInput").ap()
    d["s2mat"] = nc.dram_tensor("s2mat", [8, 64], F32, kind="ExternalInput").ap()
    d["e768"] = nc.dram_tensor("e768", [1, 256], F32, kind="ExternalInput").ap()
    return d


def emit_attention(p, din, hd):
    st = p.step
    C = hd["consts"]
    x1 = hd["x1"]

    ld = p.sem("ld")
    gld = p.sem("gld")
    sxl = p.sem("sxl")
    sTr = p.sem("sTr")
    sTrE = p.sem("sTrE")
    sWc = p.sem("sWc")
    sWcF = p.sem("sWcF")
    sQE = p.sem("sQE")
    sVt = p.sem("sVt")
    sVV = p.sem("sVV")
    sE = p.sem("sE")
    sXP = p.sem("sXP")
    sPV = p.sem("sPV")
    sUe = p.sem("sUe")
    sAT = p.sem("sAT")
    sOJ = p.sem("sOJ")
    sXt = p.sem("sXt")
    sXtF = p.sem("sXtF")
    sZ = p.sem("sZ")
    sZp = p.sem("sZp")
    sLN = p.sem("sLN")
    sSQ = p.sem("sSQ")

    st("sync", lambda s: s.dma_start(out=C["idn_bf"][:], in_=din["idn_bf"][:]),
       incs=[(ld, 16)])
    st("sync", lambda s: s.dma_start(out=C["bq"][:], in_=din["bqkv_pc"][:]),
       incs=[(ld, 16)])
    st("sync", lambda s: s.dma_start(out=C["ones_f32"][:], in_=din["ones_f32"][:]),
       incs=[(ld, 16)])
    st("gpsimd", lambda g: g.dma_start(
        out=hd["boutstage"][0:1, 0:1024],
        in_=din["bout"].rearrange("(a d) -> a d", a=1)),
       incs=[(gld, 16)])
    st("vector", lambda v: v.memset(C["ones1b"][:], 1.0), incs=[(sLN, 1)])
    st("vector", lambda v: v.memset(C["expb"][:], -40.0), incs=[(sLN, 1)])
    st("vector", lambda v: v.memset(C["epsb"][:], 1e-5), incs=[(sLN, 1)])
    st("vector", lambda v: v.memset(C["ones32"][:], 1.0 / 32.0), incs=[(sLN, 1)])

    p.open_phase()
    qT = p.sbuf("qT", [128, 8 * 2048], BF16)
    kT = p.sbuf("kT", [128, 8 * 2048], BF16)
    vv = p.sbuf("vv", [128, 16 * (16 * 65)], BF16)

    pA = p.psum("pA", [128, 1024], F32)
    pA2 = p.psum("pA2", [128, 1024], F32)

    st("tensor", lambda t: t.matmul(pA[:, 0:512], C["ones1b"][0:1, :],
                                    hd["boutstage"][0:1, 0:512],
                                    start=True, stop=True),
       waits=[(ld, None), (gld, None), (sLN, None)])
    st("tensor", lambda t: t.matmul(pA[:, 512:1024], C["ones1b"][0:1, :],
                                    hd["boutstage"][0:1, 512:1024],
                                    start=True, stop=True),
       incs=[(sE, 1)])
    st("vector", lambda v: v.tensor_copy(C["bout_bc"][:], pA[:, 0:1024]),
       waits=[(sE, None)], incs=[(sLN, 1)])
    cr = dict(ln=sLN.count, ld=ld.count, gld=gld.count)

    oj_mark = [0]
    z_mark = [0]
    zp_mark = [0]
    vr_mark = [0]

    for b in range(B):
        p.open_phase()
        pTr = p.psum("pTr", [128, 2048], BF16)
        xT = p.sbuf("xT", [128, 8 * 2048], BF16)
        xstage = p.sbuf("xstage", [128, 2 * 1024], BF16)
        NWS = 2
        wcol = p.sbuf("wcol", [128, NWS * 1024], BF16)
        vtmp = p.sbuf("vtmp", [128, 2 * 512], BF16)

        ntr = sTr.count
        for i in range(16):
            slot = i % 2
            if i >= 2:
                w = [(sTr, ntr + (i - 1) * 8)]
            elif b > 0:
                w = [(sTr, ntr), (sZ, sZ.count), (sXtF, sXtF.count),
                     (sSQ, sSQ.count), (sPV, sPV.count), (sOJ, sOJ.count)]
            else:
                w = []
            st("gpsimd", (lambda g, b=b, i=i, slot=slot: g.dma_start(
                out=xstage[:, slot * 1024:(slot + 1) * 1024],
                in_=din["x"][b, i * 128:(i + 1) * 128, :])),
               waits=w, incs=[(sxl, 16)])
            for c in range(8):
                j = i * 8 + c
                pp = (ntr + j) % 2
                w = [(sxl, sxl.count), (ld, cr["ld"])]
                if j >= 2:
                    w.append((sTrE, sTrE.count - 1))
                st("tensor", (lambda t, c=c, slot=slot, pp=pp: t.transpose(
                    pTr[:, pp * 1024: pp * 1024 + 128],
                    xstage[:, slot * 1024 + c * 128: slot * 1024 + (c + 1) * 128],
                    C["idn_bf"][:])),
                   waits=w, incs=[(sTr, 1)])
                st("vector", (lambda v, i=i, c=c, pp=pp: v.tensor_copy(
                    xT[:, c * 2048 + i * 128: c * 2048 + (i + 1) * 128],
                    pTr[:, pp * 1024: pp * 1024 + 128])),
                   waits=[(sTr, None)], incs=[(sTrE, 1)])
        xT_ready = sTrE.count

        st("vector", lambda v: v.memset(vv[:], 1.0),
           waits=[(sPV, sPV.count)], incs=[(sVV, 1)])
        vv_ones = sVV.count

        nwc0, nqe0pre = sWc.count, sQE.count
        for fout in range(24):
            slot = fout % NWS
            w = ([(sQE, nqe0pre + 4 * (fout - NWS + 1))] if fout >= NWS else [])
            st("gpsimd", (lambda g, fout=fout, slot=slot: g.dma_start(
                out=wcol[:, slot * 1024:(slot + 1) * 1024].rearrange(
                    "p (k c) -> p k c", k=8),
                in_=din["wqkv_r"][fout])),
               waits=w, incs=[(sWc, 16)])

        nqe0 = sQE.count
        for fout in range(24):
            slot = fout % NWS
            for pair in range(2):
                ps = pA if pair == 0 else pA2
                unit = fout * 2 + pair
                for k in range(8):
                    for nn in range(2):
                        n = pair * 2 + nn
                        w = []
                        if k == 0 and nn == 0:
                            w = [(sWc, nwc0 + (fout + 1) * 16), (sTrE, xT_ready)]
                            if unit >= 2:
                                w.append((sQE, nqe0 + 2 * (unit - 1)))
                            if unit < 2:
                                w.append((sLN, cr["ln"]))
                                w.append((sZp, zp_mark[0]))
                        inc = [(sE, 1)] if (k == 7 and nn == 1) else []
                        st("tensor", (lambda t, ps=ps, nn=nn, slot=slot, k=k, n=n:
                                      t.matmul(
                            ps[:, nn * 512:(nn + 1) * 512],
                            wcol[:, slot * 1024 + k * 128: slot * 1024 + (k + 1) * 128],
                            xT[:, k * 2048 + n * 512: k * 2048 + (n + 1) * 512],
                            start=(k == 0), stop=(k == 7))),
                           waits=w, incs=inc)
                mmk = sE.count
                for nn in range(2):
                    n = pair * 2 + nn
                    if fout < 8:
                        dstf = (lambda fout=fout, n=n:
                                qT[:, fout * 2048 + n * 512: fout * 2048 + (n + 1) * 512])
                    elif fout < 16:
                        dstf = (lambda fout=fout, n=n:
                                kT[:, (fout - 8) * 2048 + n * 512:
                                   (fout - 8) * 2048 + (n + 1) * 512])
                    else:
                        dstf = (lambda nn=nn: vtmp[:, nn * 512:(nn + 1) * 512])
                    w = [(sE, mmk), (ld, cr["ld"])]
                    if fout >= 16:
                        w.append((sTr, sTr.count))
                    incs = [(sQE, 1)]
                    st("scalar", (lambda s, ps=ps, nn=nn, dstf=dstf, fout=fout:
                                  s.activation(dstf(), ps[:, nn * 512:(nn + 1) * 512],
                                               AF.Identity,
                                               bias=C["bq"][:, fout:fout + 1],
                                               scale=1.0)),
                       waits=w, incs=incs)
                if fout >= 16:
                    fc = fout - 16
                    for nn in range(2):
                        n = pair * 2 + nn
                        for tsub in range(4):
                            kt = n * 4 + tsub
                            pp = sTr.count % 2
                            w = [(sQE, sQE.count), (sTrE, sTrE.count - 1)]
                            st("tensor", (lambda t, nn=nn, tsub=tsub, pp=pp: t.transpose(
                                pTr[:, pp * 1024: pp * 1024 + 128],
                                vtmp[:, nn * 512 + tsub * 128: nn * 512 + (tsub + 1) * 128],
                                C["idn_bf"][:])),
                               waits=w, incs=[(sTr, 1)])
                            for hh in range(2):
                                h = 2 * fc + hh
                                incs = [(sVV, 1)] if hh == 0 else [(sTrE, 1)]
                                st("vector", (lambda v, kt=kt, h=h, hh=hh, pp=pp:
                                              v.tensor_copy(
                                    vv[:, kt * 1040 + h * 65: kt * 1040 + h * 65 + 64],
                                    pTr[:, pp * 1024 + hh * 64: pp * 1024 + hh * 64 + 64])),
                                   waits=[(sTr, None), (sVV, vv_ones)], incs=incs)
        qkv_done = dict(qe=sQE.count, vv=sVV.count, tre=sTrE.count)
        if b == 0 and hd.get("dbg_qkv"):
            dq, dk, dv = hd["dbg_qkv"]
            sDbg = hd["sDbg"]
            st("sync", (lambda s, dq=dq: s.dma_start(out=dq, in_=qT[:])),
               waits=[(sQE, qkv_done["qe"]), (sTrE, qkv_done["tre"]),
                      (sVV, qkv_done["vv"])], incs=[(sDbg, 16)])
            st("sync", (lambda s, dk=dk: s.dma_start(out=dk, in_=kT[:])),
               incs=[(sDbg, 16)])
            st("sync", (lambda s, dv=dv: s.dma_start(out=dv, in_=vv[:])),
               incs=[(sDbg, 16)])
        p.close_phase()

        p.open_phase()
        pUs = [p.psum("pU0", [128, 512], F32), p.psum("pU1", [128, 512], F32)]
        pC = p.psum("pC", [128, 512], F32)
        ptile = p.sbuf("ptile", [128, 3 * 1024], BF16)
        attnT = p.sbuf("attnT", [128, 8 * 512], BF16)
        rrb = p.sbuf("rrb", [128, 512], BF16)
        rb_sb = p.sbuf("rb_sb", [128, 512], BF16)
        u2 = p.sbuf("u2", [128, 512], BF16)
        wout_sb = p.sbuf("wout_sb", [128, 8 * 1024], BF16)
        xtile = p.sbuf("xtile", [128, 2 * 1024], BF16)
        zbuf = p.sbuf("zbuf", [128, 1024], F32)
        s1 = p.sbuf("s1", [128, 1], F32)
        s2 = p.sbuf("s2", [128, 1], F32)
        m1 = p.sbuf("m1", [128, 1], F32)
        msq = p.sbuf("msq", [128, 1], F32)
        vr = p.sbuf("vr", [128, 1], F32)
        rstd = p.sbuf("rstd", [128, 1], F32)

        st("gpsimd", lambda g: g.dma_start(
            out=wout_sb[:].rearrange("p (k f) -> p k f", k=8),
            in_=din["wout_r"]),
           waits=[(sQE, qkv_done["qe"]), (sTrE, qkv_done["tre"]),
                  (sVV, qkv_done["vv"])],
           incs=[(gld, 16)])
        wout_ready = gld.count

        exp_marks = {}
        pv_marks = {}
        pu_free = {}
        tail_pvfin = {}
        tail_pending = [None]
        for n in range(NQ):
            for mblk in range(4):
                gidx = b * 16 + n * 4 + mblk
                slot = gidx % 2
                w = [(sXtF, gidx - 1), (sQE, qkv_done["qe"])]
                st("gpsimd", (lambda g, b=b, n=n, mblk=mblk, slot=slot: g.dma_start(
                    out=xtile[:, slot * 1024:(slot + 1) * 1024],
                    in_=din["x"][b, n * 512 + mblk * 128: n * 512 + (mblk + 1) * 128, :])),
                   waits=w, incs=[(sXt, 16)])

            for h in range(H):
                ch, hp = h // 2, h % 2
                pU = pUs[h % 2]
                e_marks = {}

                def emit_E(g):
                    pe = pA if g % 2 == 0 else pA2
                    for j in range(2):
                        kt = 2 * g + j
                        w = []
                        if j == 0:
                            w = [(sQE, qkv_done["qe"]), (sTrE, qkv_done["tre"]),
                                 (sVV, qkv_done["vv"])]
                            if (h, g - 2) in exp_marks:
                                w.append((sXP, exp_marks[(h, g - 2)]))
                            elif g < 2 and h > 0:
                                w.append((sXP, exp_marks[(h - 1, 6 + g)]))
                            elif g < 2 and h == 0:
                                w.append((sZp, zp_mark[0]))
                        st("tensor", (lambda t, pe=pe, j=j, kt=kt, ch=ch,
                                      hp=hp, n=n: t.matmul(
                            pe[:, j * 512:(j + 1) * 512],
                            kT[hp * 64:hp * 64 + 64,
                               ch * 2048 + kt * 128: ch * 2048 + (kt + 1) * 128],
                            qT[hp * 64:hp * 64 + 64,
                               ch * 2048 + n * 512: ch * 2048 + (n + 1) * 512],
                            start=True, stop=True)),
                           waits=w, incs=[(sE, 1)] if j == 1 else [])
                    e_marks[g] = sE.count

                def emit_exp(g):
                    pe = pA if g % 2 == 0 else pA2
                    pp = g % 3
                    w = [(sE, e_marks[g]), (sLN, cr["ln"])]
                    if (h, g - 3) in pv_marks:
                        w.append((sPV, pv_marks[(h, g - 3)]))
                    elif g < 3 and h > 0:
                        w.append((sPV, pv_marks[(h - 1, (6, 7, 5)[g])]))
                    st("scalar", (lambda s, pe=pe, pp=pp: s.activation(
                        ptile[:, pp * 1024:(pp + 1) * 1024], pe[:, 0:1024],
                        AF.Exp, bias=C["expb"][:, 0:1], scale=1.0)),
                       waits=w, incs=[(sXP, 1)])
                    exp_marks[(h, g)] = sXP.count

                def emit_PV(g):
                    pp = g % 3
                    for j in range(2):
                        kt = 2 * g + j
                        w = [(sXP, exp_marks[(h, g)])]
                        if g == 0 and j == 0 and (h - 2, b, n) in pu_free:
                            w.append(pu_free[(h - 2, b, n)])
                        inc = [(sPV, 1)] if j == 1 else []
                        st("tensor", (lambda t, j=j, kt=kt, h=h, pp=pp, g=g,
                                      pU=pU: t.matmul(
                            pU[0:65, :],
                            vv[:, kt * 1040 + h * 65: kt * 1040 + h * 65 + 65],
                            ptile[:, pp * 1024 + j * 512: pp * 1024 + (j + 1) * 512],
                            start=(g == 0 and j == 0), stop=(g == 7 and j == 1))),
                           waits=w, incs=inc)
                    pv_marks[(h, g)] = sPV.count

                def emit_tail(hh_, pU_):
                    ch_, hp_ = hh_ // 2, hh_ % 2
                    pvf = tail_pvfin[hh_]
                    st("vector", (lambda v, pU=pU_: v.reciprocal(
                        rrb[64:65, :], pU[64:65, :])),
                       waits=[(sPV, pvf)], incs=[(sUe, 1)])
                    st("tensor", (lambda t: t.matmul(
                        pC[0:64, :], C["ones32"][64:65, 0:64], rrb[64:65, :],
                        start=True, stop=True)),
                       waits=[(sUe, None), (sAT, sAT.count)], incs=[(sE, 1)])
                    st("vector", (lambda v: v.tensor_copy(rb_sb[0:64, :],
                                                          pC[0:64, :])),
                       waits=[(sE, None)], incs=[(sUe, 1)])
                    if hp_ == 0:
                        st("vector", (lambda v, ch=ch_, pU=pU_: v.tensor_tensor(
                            attnT[0:64, ch * 512:(ch + 1) * 512],
                            pU[0:64, :], rb_sb[0:64, :], ALU.mult)),
                           waits=[(sUe, None), (sOJ, oj_mark[0])],
                           incs=[(sAT, 1)])
                        pu_free[(hh_, b, n)] = (sAT, sAT.count)
                    else:
                        st("vector", (lambda v, pU=pU_: v.tensor_tensor(
                            u2[0:64, :], pU[0:64, :], rb_sb[0:64, :], ALU.mult)),
                           waits=[(sUe, None)], incs=[(sUe, 1)])
                        pu_free[(hh_, b, n)] = (sUe, sUe.count)
                        st("tensor", (lambda t: t.matmul(
                            pC[64:128, :], C["idn_bf"][0:64, 0:64], u2[0:64, :],
                            start=True, stop=True, tile_position=(0, 64))),
                           waits=[(sUe, None)], incs=[(sE, 1)])
                        st("vector", (lambda v, ch=ch_: v.tensor_copy(
                            attnT[64:128, ch * 512:(ch + 1) * 512],
                            pC[64:128, :])),
                           waits=[(sE, None), (sOJ, oj_mark[0])],
                           incs=[(sAT, 1)])

                emit_E(0)
                emit_exp(0)
                emit_E(1)
                emit_exp(1)
                for g in range(8):
                    if g == 7 and tail_pending[0] is not None:
                        emit_tail(*tail_pending[0])
                        tail_pending[0] = None
                    emit_PV(g)
                    if g + 2 < 8:
                        emit_E(g + 2)
                        emit_exp(g + 2)
                tail_pvfin[h] = sPV.count
                tail_pending[0] = (h, pU)
                if h == H - 1:
                    emit_tail(*tail_pending[0])
                    tail_pending[0] = None
            at_done = sAT.count
            if b == 0 and n == 0 and hd.get("dbg_at"):
                st("sync", (lambda s: s.dma_start(out=hd["dbg_at"], in_=attnT[:])),
                   waits=[(sAT, at_done)], incs=[(hd["sDbg"], 16)])

            for mblk in range(4):
                po = pA if mblk % 2 == 0 else pA2
                gidx = b * 16 + n * 4 + mblk
                for k in range(8):
                    for fo in range(2):
                        w = []
                        if k == 0 and fo == 0:
                            w = [(sAT, at_done), (gld, wout_ready)]
                            w.append((sZp, zp_mark[0] if mblk < 2
                                      else sZp.count))
                        st("tensor", (lambda t, po=po, k=k, fo=fo, mblk=mblk: t.matmul(
                            po[:, fo * 512:(fo + 1) * 512],
                            attnT[:, k * 512 + mblk * 128: k * 512 + (mblk + 1) * 128],
                            wout_sb[:, k * 1024 + fo * 512: k * 1024 + (fo + 1) * 512],
                            start=(k == 0), stop=(k == 7))),
                           waits=w, incs=[(sOJ, 1)] if (k == 7 and fo == 1) else [])
                oj_done = sOJ.count
                slot = gidx % 2
                w = [(sXt, 16 * (gidx + 1)), (sLN, cr["ln"]), (sSQ, sSQ.count)]
                if hd.get("_zw"):
                    w.append(hd.pop("_zw"))
                st("vector", (lambda v, slot=slot: v.tensor_tensor(
                    zbuf[:], xtile[:, slot * 1024:(slot + 1) * 1024],
                    C["bout_bc"][:], ALU.add)),
                   waits=w, incs=[(sXtF, 1)])
                for half in range(2):
                    st("vector", (lambda v, po=po, half=half: v.tensor_tensor(
                        zbuf[:, half * 512:(half + 1) * 512],
                        zbuf[:, half * 512:(half + 1) * 512],
                        po[:, half * 512:(half + 1) * 512], ALU.add)),
                       waits=[(sOJ, oj_done)],
                       incs=[(sZp, 1)] if half == 1 else [])
                zmark = sZp.count
                if b == 0 and n == 0 and mblk == 0 and hd.get("dbg_z"):
                    st("sync", (lambda s: s.dma_start(out=hd["dbg_z"], in_=zbuf[:])),
                       waits=[(sZ, zmark)], incs=[(hd["sDbg"], 16)])
                    st("sync", (lambda s: s.dma_start(out=hd["dbg_zx"], in_=zx[:])),
                       incs=[(hd["sDbg"], 16)])
                    hd["_zw"] = (hd["sDbg"], hd["sDbg"].count)
                st("vector", (lambda v: v.reduce_sum(s1[:, 0:1], zbuf[:], AX.X)),
                   incs=[(sLN, 1)])
                st("scalar", (lambda s: s.activation(ptile[:, 0:1024], zbuf[:],
                                                     AF.Square,
                                                     accum_out=s2[:, 0:1])),
                   waits=[(sZp, zmark), (sXP, sXP.count), (sPV, sPV.count),
                          (sLN, vr_mark[0])],
                   incs=[(sSQ, 1)])
                st("vector", (lambda v: v.tensor_scalar_mul(m1[:, 0:1], s1[:, 0:1],
                                                            1.0 / D)),
                   waits=[(sLN, None)], incs=[(sLN, 1)])
                st("vector", (lambda v: v.tensor_scalar(msq[:, 0:1], m1[:, 0:1],
                                                        m1[:, 0:1], None, ALU.mult)),
                   waits=[(sLN, None)], incs=[(sLN, 1)])
                st("vector", (lambda v: v.scalar_tensor_tensor(
                    vr[:, 0:1], s2[:, 0:1], 1.0 / D, msq[:, 0:1],
                    ALU.mult, ALU.subtract)),
                   waits=[(sSQ, None), (sLN, None)], incs=[(sLN, 1)])
                vr_mark[0] = sLN.count
                st("scalar", (lambda s: s.activation(rstd[:, 0:1], vr[:, 0:1], AF.Sqrt,
                                                     bias=C["epsb"][:, 0:1], scale=1.0)),
                   waits=[(sLN, None)], incs=[(sSQ, 1)])
                st("vector", (lambda v: v.reciprocal(rstd[:, 0:1], rstd[:, 0:1])),
                   waits=[(sSQ, None)], incs=[(sLN, 1)])
                col = gidx * 1024
                st("vector", (lambda v, col=col: v.tensor_scalar(
                    x1[:, col:col + 1024], zbuf[:], m1[:, 0:1], rstd[:, 0:1],
                    ALU.subtract, ALU.mult)),
                   waits=[(sLN, None)], incs=[(sZ, 1)])
            oj_mark[0] = sOJ.count
            z_mark[0] = sZ.count
            zp_mark[0] = sZp.count
        p.close_phase()
    p.close_phase()

    hd["attn"] = dict(z=sZ.count, sZ=sZ, sLN=sLN, ln=sLN.count)
    return hd


def emit_gating(p, din, hd):
    st = p.step
    C = hd["consts"]
    x1 = hd["x1"]
    probs, logits, ccsb = hd["probs"], hd["logits"], hd["ccsb"]
    sZ, zdone = hd["attn"]["sZ"], hd["attn"]["z"]

    sTr = p.sem("gTr")
    sTrE = p.sem("gTrE")
    sLg = p.sem("gLg")
    sLgE = p.sem("gLgE")
    sPr = p.sem("gPr")
    sPs = p.sem("gPs")
    sCt = p.sem("gCt")
    sCe = p.sem("gCe")
    gld2 = p.sem("gld2")

    idn_f32 = hd["idn_f32"]
    wg_sb = hd["wg_sb"]
    x1tr = p.sbuf("x1tr", [128, 2 * 1024], BF16)
    eall = p.sbuf("eall", [128, 256], F32)
    sall = p.sbuf("sall", [128, 32], F32)
    rall = p.sbuf("rall", [128, 32], F32)

    st("sync", lambda s: s.dma_start(out=idn_f32[:], in_=din["idn_f32"][:]),
       waits=[(sZ, zdone)], incs=[(gld2, 16)])
    st("gpsimd", lambda g: g.dma_start(
        out=wg_sb[:].rearrange("p (k e) -> p k e", k=8),
        in_=din["wg_r"]),
       waits=[(sZ, zdone)], incs=[(gld2, 16)])

    pTr = p.psum("gpTr", [128, 2048], BF16)
    pL = p.psum("gpL", [128, 2048], F32)
    pT8 = p.psum("gpT8", [128, 1024], F32)

    for i in range(32):
        slot = i % 2
        for c in range(8):
            j = i * 8 + c
            pp = j % 2
            w = [(sZ, zdone)]
            if j >= 2:
                w.append((sTrE, sTrE.count - 1))
            if c == 0 and i >= 2:
                w.append((sLg, sLg.count - 1))
            st("tensor", (lambda t, i=i, c=c, pp=pp: t.transpose(
                pTr[:, pp * 1024: pp * 1024 + 128],
                x1[:, i * 1024 + c * 128: i * 1024 + (c + 1) * 128],
                C["idn_bf"][:])),
               waits=w, incs=[(sTr, 1)])
            st("vector", (lambda v, slot=slot, c=c, pp=pp: v.tensor_copy(
                x1tr[:, slot * 1024 + c * 128: slot * 1024 + (c + 1) * 128],
                pTr[:, pp * 1024: pp * 1024 + 128])),
               waits=[(sTr, None)], incs=[(sTrE, 1)])
        tr_mark = sTrE.count
        ppl = i % 4
        for k in range(8):
            w = []
            if k == 0:
                w = [(sTrE, tr_mark), (gld2, None)]
                if i >= 4:
                    w.append((sLgE, sLgE.count - 3))
            st("tensor", (lambda t, ppl=ppl, k=k, slot=slot: t.matmul(
                pL[:, ppl * 512: ppl * 512 + 8],
                x1tr[:, slot * 1024 + k * 128: slot * 1024 + (k + 1) * 128],
                wg_sb[:, k * 8:(k + 1) * 8],
                start=(k == 0), stop=(k == 7))),
               waits=w, incs=[(sLg, 1)] if k == 7 else [])
        st("vector", (lambda v, i=i, ppl=ppl: v.tensor_copy(
            logits[:, i * 8:(i + 1) * 8], pL[:, ppl * 512: ppl * 512 + 8])),
           waits=[(sLg, None)], incs=[(sLgE, 1)])

    st("scalar", (lambda s: s.activation(eall[:], logits[:], AF.Exp)),
       waits=[(sLgE, None)], incs=[(sPs, 1)])
    st("vector", (lambda v: v.tensor_reduce(
        sall[:], eall[:].rearrange("p (i e) -> p i e", e=8), AX.X, ALU.add)),
       waits=[(sPs, None)], incs=[(sPr, 1)])
    st("vector", (lambda v: v.reciprocal(rall[:], sall[:])),
       waits=[(sPr, None)], incs=[(sPr, 1)])
    for i in range(32):
        st("vector", (lambda v, i=i: v.tensor_scalar(
            probs[:, i * 8:(i + 1) * 8], eall[:, i * 8:(i + 1) * 8],
            rall[:, i:i + 1], None, ALU.mult)),
           waits=[(sPr, None)], incs=[(sPr, 1)])
    for i in range(32):
        ppt = i % 2
        st("tensor", (lambda t, i=i, ppt=ppt: t.transpose(
            pT8[0:8, ppt * 512: ppt * 512 + 128], probs[:, i * 8:(i + 1) * 8],
            idn_f32[:])),
           waits=[(sPr, None), (sCe, sCe.count - 1)], incs=[(sCt, 1)])
        st("vector", (lambda v, i=i, ppt=ppt: v.tensor_copy(
            ccsb[0:8, i * 128:(i + 1) * 128], pT8[0:8, ppt * 512: ppt * 512 + 128])),
           waits=[(sCt, None)], incs=[(sCe, 1)])

    hd["gating"] = dict(ce=sCe.count, pr=sPr.count, sCe=sCe, sPr=sPr)
    return hd


def emit_routing(p, din, hd, fused=True):
    nc = p.nc
    st = p.step
    C = hd["consts"]
    probs = hd["probs"]
    gates = hd["gates"]
    ce_done = hd["gating"]["ce"]
    sCe = hd["gating"]["sCe"]
    sPr2 = hd["gating"]["sPr"]
    NITER = 32

    sCC = p.sem("rCC")
    sGt = p.sem("rGt")
    sGm = p.sem("rGm")
    sDs = p.sem("rDs")
    sAG = p.sem("rAG")
    sPG = p.sem("rPG")
    sBi = p.sem("rBi")
    sBm = p.sem("rBm")
    sRt = p.sem("rRt")

    smat_sb = p.sbuf("smat", [64, 8], F32)
    s2mat_sb = p.sbuf("s2mat", [8, 64], F32)
    pg = p.sbuf("pg", [64, T], F32)
    trash64 = p.sbuf("trash64", [64, T], BF16)
    cnt = p.sbuf("cnt", [64, 1], F32)
    thv = p.sbuf("thv", [64, 1], F32)
    lo = [p.sbuf("lo_a", [8, 1], F32), p.sbuf("lo_b", [8, 1], F32)]
    hi = [p.sbuf("hi_a", [8, 1], F32), p.sbuf("hi_b", [8, 1], F32)]
    mid = p.sbuf("mid", [8, 1], F32)
    midt = p.sbuf("midt", [8, 1], F32)
    pred = p.sbuf("pred", [8, 1], F32)
    npred = p.sbuf("npred", [8, 1], F32)
    t1 = p.sbuf("t1", [8, 1], F32)
    t2 = p.sbuf("t2", [8, 1], F32)
    throw = p.sbuf("throw", [1, 8], F32)
    th8 = hd["th8"]
    mx_a = p.sbuf("mx_a", [128, 32], F32)
    wvals = hd["wvals"]
    dsti = hd["dsti"]
    gT = p.sbuf("gT", [8, T], F32)
    possT = p.sbuf("possT", [8, T], F32)
    zer8 = p.sbuf("zer8", [8, T], F32)
    postok_a = p.sbuf("postok_a", [128, 256], F32)
    e768_a = p.sbuf("e768_a", [128, 256], F32)
    clamp_a = p.sbuf("clamp_a", [128, 256], F32)
    selb_a = p.sbuf("selb_a", [128, 256], F32)
    score_a = p.sbuf("score_a", [128, 256], F32)
    oh_a = p.sbuf("oh_a", [128, 256], F32)

    pRt = p.psum("pRt", [128, 1024], F32)
    pOh = p.psum("pOh", [128, 1024], F32)

    st("sync", lambda s: s.dma_start(out=smat_sb[:], in_=din["smat"][:]),
       waits=[(sCe, ce_done), (sPr2, hd["gating"]["pr"])], incs=[(sCC, 16)])
    st("sync", lambda s: s.dma_start(out=s2mat_sb[:], in_=din["s2mat"][:]),
       incs=[(sCC, 16)])

    if fused:
        cc_in = nc.dram_tensor("cc_in", [8, T], F32, kind="Internal").ap()
        cc_out = nc.dram_tensor("cc_out", [64, T], F32, kind="Internal",
                                addr_space="Shared").ap()
        st("sync", lambda s: s.dma_start(out=cc_in[:, :], in_=hd["ccsb"][:]),
           waits=[(sCe, ce_done)], incs=[(sCC, 16)])
        st("gpsimd", lambda g: g.collective_compute(
            "AllGather", ALU.bypass, replica_groups=[[0, 1, 2, 3, 4, 5, 6, 7]],
            ins=[cc_in[:, :]], outs=[cc_out[:, :]]),
           waits=[(sCC, None)], incs=[(sAG, 1)])
        st("sync", lambda s: s.dma_start(out=pg[:], in_=cc_out[:, :]),
           waits=[(sAG, 1)], incs=[(sPG, 16)])
    else:
        probs_all = nc.dram_tensor("probs_all", [64, T], F32,
                                   kind="ExternalInput").ap()
        st("sync", lambda s: s.dma_start(out=pg[:], in_=probs_all[:]),
           waits=[(sCC, None)], incs=[(sPG, 16)])

    st("vector", lambda v: v.memset(lo[0][:], 0.0), incs=[(sBi, 1)])
    st("vector", lambda v: v.memset(hi[0][:], 1.0), incs=[(sBi, 1)])
    st("sync", lambda s: s.dma_start(out=e768_a[0:1, :], in_=din["e768"][0:1, :]),
       waits=[(sCe, ce_done)], incs=[(sCC, 16)])
    st("tensor", lambda t: t.matmul(pOh[0:128, 0:256], C["ones_f32"][0:1, :],
                                    e768_a[0:1, 0:256], start=True, stop=True),
       waits=[(sCC, None)], incs=[(sGm, 1)])
    st("vector", lambda v: v.tensor_copy(e768_a[:], pOh[0:128, 0:256]),
       waits=[(sGm, None)], incs=[(sDs, 1)])
    st("vector", lambda v: v.tensor_scalar(clamp_a[:], e768_a[:], 767.0, None,
                                           ALU.add),
       waits=[(sDs, None)], incs=[(sDs, 1)])

    for it in range(NITER):
        a, b2 = it % 2, (it + 1) % 2
        bank = it % 2
        st("vector", (lambda v, a=a: v.tensor_tensor(midt[:], lo[a][:], hi[a][:],
                                                     ALU.add)),
           waits=[(sBi, None)], incs=[(sBi, 1)])
        st("vector", lambda v: v.tensor_scalar_mul(mid[:], midt[:], 0.5),
           waits=[(sBi, None)], incs=[(sBi, 1)])
        st("tensor", (lambda t, bank=bank: t.matmul(
            pRt[0:64, bank * 512: bank * 512 + 1], s2mat_sb[:], mid[:],
            start=True, stop=True)),
           waits=[(sBi, None), (sCC, None), (sCe, ce_done)], incs=[(sBm, 1)])
        st("vector", (lambda v, bank=bank: v.tensor_copy(
            thv[:], pRt[0:64, bank * 512: bank * 512 + 1])),
           waits=[(sBm, None)], incs=[(sBi, 1)])
        st("vector", (lambda v: v.tensor_scalar(
            trash64[:], pg[:], thv[0:64, 0:1], 0.0, ALU.is_ge, ALU.add,
            accum_out=cnt[0:64, 0:1])),
           waits=[(sBi, None), (sPG, None)], incs=[(sBi, 1)])
        st("tensor", (lambda t, bank=bank: t.matmul(
            pRt[0:8, bank * 512 + 256: bank * 512 + 257], smat_sb[:],
            cnt[0:64, 0:1], start=True, stop=True)),
           waits=[(sBi, None)], incs=[(sBm, 1)])
        st("vector", (lambda v, bank=bank: v.tensor_scalar(
            pred[:], pRt[0:8, bank * 512 + 256: bank * 512 + 257], 4096.0, None,
            ALU.is_gt)),
           waits=[(sBm, None)], incs=[(sBi, 1)])
        st("vector", lambda v: v.tensor_scalar(npred[:], pred[:], -1.0, 1.0,
                                               ALU.mult, ALU.add),
           waits=[(sBi, None)], incs=[(sBi, 1)])
        st("vector", (lambda v, a=a: v.tensor_scalar(t1[:], lo[a][:],
                                                     npred[0:8, 0:1], None,
                                                     ALU.mult)),
           waits=[(sBi, None)], incs=[(sBi, 1)])
        st("vector", (lambda v, b2=b2: v.scalar_tensor_tensor(
            lo[b2][:], mid[:], pred[0:8, 0:1], t1[:], ALU.mult, ALU.add)),
           waits=[(sBi, None)], incs=[(sBi, 1)])
        st("vector", (lambda v, a=a: v.tensor_scalar(t2[:], hi[a][:],
                                                     pred[0:8, 0:1], None,
                                                     ALU.mult)),
           waits=[(sBi, None)], incs=[(sBi, 1)])
        st("vector", (lambda v, b2=b2: v.scalar_tensor_tensor(
            hi[b2][:], mid[:], npred[0:8, 0:1], t2[:], ALU.mult, ALU.add)),
           waits=[(sBi, None)], incs=[(sBi, 1)])
    fin = NITER % 2
    st("tensor", (lambda t, fin=fin: t.matmul(
        pRt[0:1, 0:8], hi[fin][:], hd["idn_f32"][0:8, 0:8], start=True,
        stop=True)),
       waits=[(sBi, None)], incs=[(sBm, 1)])
    st("vector", lambda v: v.tensor_copy(throw[:], pRt[0:1, 0:8]),
       waits=[(sBm, None)], incs=[(sBi, 1)])
    st("tensor", lambda t: t.matmul(pRt[0:128, 512:520], C["ones_f32"][0:1, :],
                                    throw[:], start=True, stop=True),
       waits=[(sBi, None)], incs=[(sBm, 1)])
    st("vector", lambda v: v.tensor_copy(th8[:], pRt[0:128, 512:520]),
       waits=[(sBm, None)], incs=[(sBi, 1)])
    th_done = sBi.count

    def b8(ap_small):
        return ap_small[:].rearrange("p (i o) -> p i o", o=1).to_broadcast(
            [128, 32, 8])

    th8b = th8[:].rearrange("p (i o) -> p i o", o=8).to_broadcast([128, 32, 8])
    pr3 = probs[:].rearrange("p (i e) -> p i e", e=8)
    st("vector", (lambda v: v.tensor_tensor(
        selb_a[:].rearrange("p (i e) -> p i e", e=8), pr3, th8b, ALU.is_ge)),
       waits=[(sBi, th_done), (sPr2, hd["gating"]["pr"])], incs=[(sRt, 1)])
    st("vector", (lambda v: v.scalar_tensor_tensor(
        score_a[:], selb_a[:], 2.0, probs[:], ALU.mult, ALU.add)),
       waits=[(sRt, None)], incs=[(sRt, 1)])
    st("vector", (lambda v: v.tensor_reduce(
        mx_a[:], score_a[:].rearrange("p (i e) -> p i e", e=8), AX.X, ALU.max)),
       waits=[(sRt, None)], incs=[(sRt, 1)])
    st("vector", (lambda v: v.tensor_tensor(
        oh_a[:].rearrange("p (i e) -> p i e", e=8),
        score_a[:].rearrange("p (i e) -> p i e", e=8), b8(mx_a), ALU.is_equal)),
       waits=[(sRt, None)], incs=[(sRt, 1)])
    st("vector", (lambda v: v.tensor_tensor(
        score_a[:], probs[:], oh_a[:], ALU.mult)),
       waits=[(sRt, None)], incs=[(sRt, 1)])
    st("vector", (lambda v: v.tensor_reduce(
        wvals[:], score_a[:].rearrange("p (i e) -> p i e", e=8), AX.X, ALU.max)),
       waits=[(sRt, None)], incs=[(sRt, 1)])
    st("vector", (lambda v: v.tensor_tensor(
        gates[:].rearrange("p (i e) -> p i e", e=8),
        oh_a[:].rearrange("p (i e) -> p i e", e=8), b8(wvals), ALU.mult)),
       waits=[(sRt, None)], incs=[(sRt, 1)])
    gates_done = sRt.count
    for i in range(32):
        ppo = i % 2
        st("tensor", (lambda t, i=i, ppo=ppo: t.transpose(
            pOh[0:8, ppo * 512: ppo * 512 + 128], gates[:, i * 8:(i + 1) * 8],
            hd["idn_f32"][:])),
           waits=[(sRt, gates_done), (sGt, sGt.count - 1), (sDs, 2)],
           incs=[(sGm, 1)])
        st("vector", (lambda v, i=i, ppo=ppo: v.tensor_copy(
            gT[0:8, i * 128:(i + 1) * 128],
            pOh[0:8, ppo * 512: ppo * 512 + 128])),
           waits=[(sGm, None)], incs=[(sGt, 1)])
    gt_done = sGt.count

    st("vector", lambda v: v.memset(zer8[:], 0.0), incs=[(sBi, 1)])
    st("vector", (lambda v: v.tensor_scalar(possT[:], gT[:], 0.0, None,
                                            ALU.is_gt)),
       waits=[(sGt, gt_done), (sBi, None)], incs=[(sBi, 1)])
    st("vector", (lambda v: v.tensor_tensor_scan(
        possT[:], possT[:], zer8[:], 0.0, ALU.add, ALU.add)),
       waits=[(sBi, None)], incs=[(sBi, 1)])
    scan_done = sBi.count

    for i in range(32):
        ppo = i % 2
        st("tensor", (lambda t, i=i, ppo=ppo: t.transpose(
            pOh[0:128, ppo * 512: ppo * 512 + 8],
            possT[0:8, i * 128:(i + 1) * 128], hd["idn_f32"][0:8, 0:8])),
           waits=[(sBi, scan_done), (sDs, sDs.count - 1)], incs=[(sGm, 1)])
        st("vector", (lambda v, i=i, ppo=ppo: v.tensor_copy(
            postok_a[:, i * 8:(i + 1) * 8],
            pOh[0:128, ppo * 512: ppo * 512 + 8])),
           waits=[(sGm, None)], incs=[(sDs, 1)])
    pos_done = sDs.count
    st("vector", (lambda v: v.scalar_tensor_tensor(
        postok_a[:], postok_a[:], 1.0, e768_a[:], ALU.subtract, ALU.add)),
       waits=[(sDs, pos_done)], incs=[(sDs, 1)])
    st("vector", (lambda v: v.tensor_tensor(postok_a[:], postok_a[:],
                                            clamp_a[:], ALU.min)),
       waits=[(sDs, None)], incs=[(sDs, 1)])
    st("vector", (lambda v: v.tensor_tensor(postok_a[:], postok_a[:], oh_a[:],
                                            ALU.mult)),
       waits=[(sDs, None)], incs=[(sDs, 1)])
    st("vector", (lambda v: v.tensor_reduce(
        dsti[:], postok_a[:].rearrange("p (i e) -> p i e", e=8), AX.X,
        ALU.add)),
       waits=[(sDs, None)], incs=[(sDs, 1)])
    if hd.get("dbg_rt"):
        dth, dga, dpg = hd["dbg_rt"]
        st("sync", (lambda s: s.dma_start(out=dth, in_=th8[:])),
           waits=[(sBi, th_done)], incs=[(hd["sDbg"], 16)])
        st("sync", (lambda s: s.dma_start(out=dga, in_=gates[:])),
           waits=[(sRt, sRt.count)], incs=[(hd["sDbg"], 16)])
        st("sync", (lambda s: s.dma_start(out=dpg, in_=pg[:])),
           waits=[(sPG, None)], incs=[(hd["sDbg"], 16)])
        st("sync", None, waits=[(hd["sDbg"], hd["sDbg"].count)])
    if hd.get("dbg_rt2"):
        st("sync", (lambda s: s.dma_start(out=hd["dbg_rt2"][0], in_=gT[:])),
           waits=[(sGt, gt_done)], incs=[(hd["sDbg"], 16)])
        st("sync", (lambda s: s.dma_start(out=hd["dbg_rt2"][1], in_=possT[:])),
           waits=[(sDs, sDs.count)], incs=[(hd["sDbg"], 16)])
    hd["routing"] = dict(rt=sRt.count, sRt=sRt, sBi=sBi, bi=sBi.count, sPG=sPG,
                         sDs=sDs, ds=sDs.count)
    return hd


def emit_ffn_dense(p, din, hd):
    st = p.step
    C = hd["consts"]
    x1 = hd["x1"]
    gates = hd["gates"]
    sRt, rt_done = hd["routing"]["sRt"], hd["routing"]["rt"]
    sBi, bi_done = hd["routing"]["sBi"], hd["routing"]["bi"]
    sPG = hd["routing"]["sPG"]

    sW1 = p.sem("fW1")
    sW2 = p.sem("fW2")
    sTr3 = p.sem("fTr")
    sTrE3 = p.sem("fTrE")
    sS1 = p.sem("fS1")
    sGl = p.sem("fGl")
    sS2 = p.sem("fS2")
    sGE = p.sem("fGE")
    sLN2 = p.sem("fLN2")
    sSQ2 = p.sem("fSQ2")
    sOut = p.sem("fOut")

    x1Tg = p.sbuf("x1Tg", [128, 8 * 1024], BF16)
    w1s = p.sbuf("w1s", [128, 2 * 2048], BF16)
    w2s = p.sbuf("w2s", [128, 2 * 8192], BF16)
    geluT = p.sbuf("geluT", [128, 16 * 1024], BF16)
    outt = p.sbuf("outt", [128, 2 * 1024], F32)
    s1 = p.sbuf("f_s1", [128, 1], F32)
    s2 = p.sbuf("f_s2", [128, 1], F32)
    m1 = p.sbuf("f_m1", [128, 1], F32)
    msq = p.sbuf("f_msq", [128, 1], F32)
    vr = p.sbuf("f_vr", [128, 1], F32)
    rstd = p.sbuf("f_rstd", [128, 1], F32)
    trln = p.sbuf("f_trln", [128, 1024], BF16)

    pF = p.psum("pF", [128, 2048], F32)
    pTr3 = p.psum("fpTr", [128, 2048], BF16)

    y = p.nc.dram_tensor("y", [B, S, D], F32, kind="ExternalOutput").ap()

    w1ld = [0]
    w2ld = [0]
    for grp in range(4):
        ntr = sTr3.count
        for mm_ in range(8):
            gidx = grp * 8 + mm_
            for c in range(8):
                j = sTr3.count - ntr
                pp = sTr3.count % 2
                w = [(sRt, rt_done)]
                if j >= 2:
                    w.append((sTrE3, sTrE3.count - 1))
                if j == 0:
                    w.append((sS1, sS1.count))
                st("tensor", (lambda t, gidx=gidx, c=c, pp=pp: t.transpose(
                    pTr3[:, pp * 1024: pp * 1024 + 128],
                    x1[:, gidx * 1024 + c * 128: gidx * 1024 + (c + 1) * 128],
                    C["idn_bf"][:])),
                   waits=w, incs=[(sTr3, 1)])
                st("vector", (lambda v, mm_=mm_, c=c, pp=pp: v.tensor_copy(
                    x1Tg[:, c * 1024 + mm_ * 128: c * 1024 + (mm_ + 1) * 128],
                    pTr3[:, pp * 1024: pp * 1024 + 128])),
                   waits=[(sTr3, None)], incs=[(sTrE3, 1)])
        trg_done = sTrE3.count

        for e in range(E):
            for f2cg in range(8):
                slot = w1ld[0] % 2
                w = [(sBi, bi_done), (sPG, 16)]
                if w1ld[0] >= 2:
                    w.append((sS1, sS1.count - 1))
                st("gpsimd", (lambda g, e=e, f2cg=f2cg, slot=slot: g.dma_start(
                    out=w1s[:, slot * 2048:(slot + 1) * 2048].rearrange(
                        "p (k c) -> p k c", k=8),
                    in_=din["w1r"][e, f2cg])),
                   waits=w, incs=[(sW1, 16)])
                w1ld[0] += 1
                ldmark = sW1.count
                for k in range(8):
                    for j in range(2):
                        for sub in range(2):
                            bk = j * 2 + sub
                            w = []
                            if k == 0 and j == 0 and sub == 0:
                                w = [(sW1, ldmark), (sTrE3, trg_done),
                                     (sGE, sGE.count), (sGl, sGl.count)]
                            st("tensor", (lambda t, slot=slot, j=j, sub=sub,
                                          bk=bk, k=k: t.matmul(
                                pF[:, bk * 512:(bk + 1) * 512],
                                w1s[:, slot * 2048 + k * 256 + j * 128:
                                    slot * 2048 + k * 256 + (j + 1) * 128],
                                x1Tg[:, k * 1024 + sub * 512:
                                     k * 1024 + (sub + 1) * 512],
                                start=(k == 0), stop=(k == 7))),
                               waits=w,
                               incs=[(sS1, 1)] if (k == 7 and bk == 3) else [])
                s1_mark = sS1.count
                for j in range(2):
                    f2c = f2cg * 2 + j
                    st("scalar", (lambda s, j=j, f2c=f2c: s.activation(
                        geluT[:, f2c * 1024:(f2c + 1) * 1024],
                        pF[:, j * 1024:(j + 1) * 1024], AF.Gelu)),
                       waits=[(sS1, s1_mark)], incs=[(sGl, 1)])
            gl_mark = sGl.count
            if grp == 0 and e == 0 and hd.get("dbg_ffn"):
                st("sync", (lambda s: s.dma_start(out=hd["dbg_ffn"][0],
                                                  in_=geluT[:])),
                   waits=[(sGl, gl_mark)], incs=[(hd["sDbg"], 16)])
                hd["_fw"] = True
            for half in range(2):
                slot = w2ld[0] % 2
                w = []
                if hd.pop("_fw", False):
                    w.append((hd["sDbg"], hd["sDbg"].count))
                if w2ld[0] >= 2:
                    w.append((sS2, sS2.count - 7))
                st("gpsimd", (lambda g, e=e, half=half, slot=slot: g.dma_start(
                    out=w2s[:, slot * 8192:(slot + 1) * 8192].rearrange(
                        "p (c f) -> p c f", c=16),
                    in_=din["w2r"][e, half])),
                   waits=w, incs=[(sW2, 16)])
                w2ld[0] += 1
                ld2mark = sW2.count
                for m in range(8):
                    gidx = grp * 8 + m
                    pp = m % 2
                    for f2c in range(16):
                        w = []
                        if f2c == 0:
                            w = [(sW2, ld2mark), (sGl, gl_mark)]
                            if not (m < 2 and half == 0 and e == 0 and grp == 0):
                                w.append((sGE, sGE.count - 1))
                        st("tensor", (lambda t, pp=pp, f2c=f2c, m=m, half=half,
                                      slot=slot: t.matmul(
                            pF[:, pp * 512: (pp + 1) * 512],
                            geluT[:, f2c * 1024 + m * 128: f2c * 1024 + (m + 1) * 128],
                            w2s[:, slot * 8192 + f2c * 512:
                                slot * 8192 + (f2c + 1) * 512],
                            start=(f2c == 0), stop=(f2c == 15))),
                           waits=w, incs=[(sS2, 1)] if f2c == 15 else [])
                    s2m = sS2.count
                    st("vector", (lambda v, gidx=gidx, e=e, pp=pp, half=half:
                                  v.scalar_tensor_tensor(
                        x1[:, gidx * 1024 + half * 512:
                           gidx * 1024 + half * 512 + 512],
                        pF[:, pp * 512: (pp + 1) * 512],
                        gates[:, gidx * 8 + e: gidx * 8 + e + 1],
                        x1[:, gidx * 1024 + half * 512:
                           gidx * 1024 + half * 512 + 512],
                        ALU.mult, ALU.add)),
                       waits=[(sS2, s2m), (sRt, rt_done)], incs=[(sGE, 1)])
    ffn_done = sGE.count
    if hd.get("dbg_ffn"):
        st("sync", (lambda s: s.dma_start(out=hd["dbg_ffn"][1], in_=x1[:])),
           waits=[(sGE, ffn_done)], incs=[(hd["sDbg"], 16)])
        st("sync", None, waits=[(hd["sDbg"], hd["sDbg"].count)])

    for gidx in range(32):
        pp = gidx % 2
        w = [(sGE, ffn_done), (sSQ2, sSQ2.count)]
        if gidx >= 2:
            w.append((sOut, (gidx - 1) * 16))
        st("vector", (lambda v, gidx=gidx: v.reduce_sum(
            s1[:, 0:1], x1[:, gidx * 1024:(gidx + 1) * 1024], AX.X)),
           waits=w, incs=[(sLN2, 1)])
        st("scalar", (lambda s, gidx=gidx: s.activation(
            trln[:], x1[:, gidx * 1024:(gidx + 1) * 1024], AF.Square,
            accum_out=s2[:, 0:1])),
           waits=[(sGE, ffn_done), (sLN2, sLN2.count - 1)], incs=[(sSQ2, 1)])
        st("vector", (lambda v: v.tensor_scalar_mul(m1[:, 0:1], s1[:, 0:1],
                                                    1.0 / D)),
           waits=[(sLN2, None)], incs=[(sLN2, 1)])
        st("vector", (lambda v: v.tensor_scalar(msq[:, 0:1], m1[:, 0:1],
                                                m1[:, 0:1], None, ALU.mult)),
           waits=[(sLN2, None)], incs=[(sLN2, 1)])
        st("vector", (lambda v: v.scalar_tensor_tensor(
            vr[:, 0:1], s2[:, 0:1], 1.0 / D, msq[:, 0:1], ALU.mult,
            ALU.subtract)),
           waits=[(sSQ2, None), (sLN2, None)], incs=[(sLN2, 1)])
        st("scalar", (lambda s: s.activation(rstd[:, 0:1], vr[:, 0:1], AF.Sqrt,
                                             bias=C["epsb"][:, 0:1], scale=1.0)),
           waits=[(sLN2, None)], incs=[(sSQ2, 1)])
        st("vector", (lambda v: v.reciprocal(rstd[:, 0:1], rstd[:, 0:1])),
           waits=[(sSQ2, None)], incs=[(sLN2, 1)])
        st("vector", (lambda v, gidx=gidx, pp=pp: v.tensor_scalar(
            outt[:, pp * 1024:(pp + 1) * 1024],
            x1[:, gidx * 1024:(gidx + 1) * 1024], m1[:, 0:1], rstd[:, 0:1],
            ALU.subtract, ALU.mult)),
           waits=[(sLN2, None)], incs=[(sLN2, 1)])
        b_, row = gidx // 16, (gidx % 16) * 128
        st("sync", (lambda s, b_=b_, row=row, pp=pp: s.dma_start(
            out=y[b_, row:row + 128, :], in_=outt[:, pp * 1024:(pp + 1) * 1024])),
           waits=[(sLN2, None)], incs=[(sOut, 16)])
    st("sync", None, waits=[(sOut, 32 * 16)])
    st("gpsimd", None, waits=[(sOut, 32 * 16)])
    st("vector", None, waits=[(sOut, 32 * 16)])
    return hd




def emit_ffn_sparse(p, din, hd):
    st = p.step
    nc = p.nc
    C = hd["consts"]
    x1 = hd["x1"]
    wvals = hd["wvals"]
    dsti = hd["dsti"]
    sRt, rt_done = hd["routing"]["sRt"], hd["routing"]["rt"]
    sBi, bi_done = hd["routing"]["sBi"], hd["routing"]["bi"]
    sPG = hd["routing"]["sPG"]
    sDs, ds_done = hd["routing"]["sDs"], hd["routing"]["ds"]
    CP = 768
    MT = CP // 128

    sCast = p.sem("sCast")
    sZt = p.sem("sZt")
    sZe = p.sem("sZe")
    sSc = p.sem("sSc")
    sXg = p.sem("sXg")
    sTr3 = p.sem("fTr")
    sTrE3 = p.sem("fTrE")
    sW1 = p.sem("fW1")
    sW2 = p.sem("fW2")
    sS1 = p.sem("fS1")
    sGl = p.sem("fGl")
    sS2 = p.sem("fS2")
    sHe = p.sem("fHe")
    sHs = p.sem("fHs")
    sHg = p.sem("fHg")
    sLN2 = p.sem("fLN2")
    sSQ2 = p.sem("fSQ2")
    sOut = p.sem("fOut")

    xgd = nc.dram_tensor("xgd", [E * CP, D], BF16, kind="Internal").ap()
    hdd = nc.dram_tensor("hdd", [E * CP, D], BF16, kind="Internal").ap()

    dstint = p.sbuf("dstint", [128, 32], mybir.dt.int32)
    zt = p.sbuf("zt", [128, 1024], BF16)
    xgs = p.sbuf("xgs", [128, 2 * 1024], BF16)
    xgT = p.sbuf("xgT", [128, 8 * CP], BF16)
    w1s = p.sbuf("w1s", [128, 2 * 2048], BF16)
    w2s = p.sbuf("w2s", [128, 2 * 8192], BF16)
    geluT = p.sbuf("geluT", [128, 16 * CP], BF16)
    hbuf = p.sbuf("hbuf", [128, 2 * 1024], BF16)
    hgs = p.sbuf("hgs", [128, 4 * 1024], BF16)
    z2b = p.sbuf("z2b", [128, 1024], F32)
    outt = p.sbuf("outt", [128, 2 * 1024], F32)
    s1 = p.sbuf("f_s1", [128, 1], F32)
    s2 = p.sbuf("f_s2", [128, 1], F32)
    m1 = p.sbuf("f_m1", [128, 1], F32)
    msq = p.sbuf("f_msq", [128, 1], F32)
    vr = p.sbuf("f_vr", [128, 1], F32)
    rstd = p.sbuf("f_rstd", [128, 1], F32)

    pF = p.psum("pF", [128, 2048], F32)
    pTr3 = p.psum("fpTr", [128, 2048], BF16)

    y = nc.dram_tensor("y", [B, S, D], F32, kind="ExternalOutput").ap()

    st("vector", (lambda v: v.memset(zt[:], 0.0)), incs=[(sZt, 1)])
    st("vector", (lambda v: v.tensor_copy(dstint[:], dsti[:])),
       waits=[(sDs, ds_done)], incs=[(sCast, 1)])
    if hd.get("dbg_sp"):
        st("sync", (lambda s: s.dma_start(out=hd["dbg_sp"][0], in_=dsti[:])),
           waits=[(sDs, ds_done)], incs=[(hd["sDbg"], 16)])

    for r in range(E * CP // 128):
        st("sync", (lambda s, r=r: s.dma_start(
            out=xgd[r * 128:(r + 1) * 128, :], in_=zt[:])),
           waits=[(sZt, 1)], incs=[(sZe, 16)])
    ze_done = sZe.count

    for i in range(32):
        st("gpsimd", (lambda g, i=i: g.indirect_dma_start(
            xgd[0:128, :], IndirectOffsetOnAxis(dstint[:, i:i + 1], 0),
            x1[:, i * 1024:(i + 1) * 1024], None)),
           waits=[(sCast, 1), (sZe, ze_done)], incs=[(sSc, 16)])
    sc_done = sSc.count
    if hd.get("dbg_sp"):
        st("sync", (lambda s: s.dma_start(out=hd["dbg_sp"][1],
                                          in_=xgd[0:256, :])),
           waits=[(sSc, sc_done)], incs=[(hd["sDbg"], 16)])

    w1ld = [0]
    w2ld = [0]
    for e in range(E):
        for m in range(MT):
            slot = (e * MT + m) % 2
            w = [(sSc, sc_done)]
            if e * MT + m >= 2:
                w.append((sTr3, sTr3.count - 8))
            st("sync", (lambda s, e=e, m=m, slot=slot: s.dma_start(
                out=xgs[:, slot * 1024:(slot + 1) * 1024],
                in_=xgd[e * CP + m * 128: e * CP + (m + 1) * 128, :])),
               waits=w, incs=[(sXg, 16)])
            for c in range(8):
                ppt = sTr3.count % 2
                w = [(sXg, sXg.count)]
                if sTr3.count >= 2:
                    w.append((sTrE3, sTrE3.count - 1))
                if e > 0 and m == 0 and c == 0:
                    w.append((sS1, sS1.count))
                st("tensor", (lambda t, slot=slot, c=c, ppt=ppt: t.transpose(
                    pTr3[:, ppt * 1024: ppt * 1024 + 128],
                    xgs[:, slot * 1024 + c * 128: slot * 1024 + (c + 1) * 128],
                    C["idn_bf"][:])),
                   waits=w, incs=[(sTr3, 1)])
                st("vector", (lambda v, m=m, c=c, ppt=ppt: v.tensor_copy(
                    xgT[:, c * CP + m * 128: c * CP + (m + 1) * 128],
                    pTr3[:, ppt * 1024: ppt * 1024 + 128])),
                   waits=[(sTr3, None)], incs=[(sTrE3, 1)])
        trg_done = sTrE3.count

        for f2cg in range(8):
            slot = w1ld[0] % 2
            w = [(sBi, bi_done), (sPG, 16)]
            if w1ld[0] >= 2:
                w.append((sS1, sS1.count - 1))
            st("gpsimd", (lambda g, e=e, f2cg=f2cg, slot=slot: g.dma_start(
                out=w1s[:, slot * 2048:(slot + 1) * 2048].rearrange(
                    "p (k c) -> p k c", k=8),
                in_=din["w1r"][e, f2cg])),
               waits=w, incs=[(sW1, 16)])
            w1ld[0] += 1
            ldmark = sW1.count
            for k in range(8):
                for j in range(2):
                    for sub in range(2):
                        bk = j * 2 + sub
                        w = []
                        if k == 0 and j == 0 and sub == 0:
                            w = [(sW1, ldmark), (sTrE3, trg_done),
                                 (sHe, sHe.count), (sGl, sGl.count)]
                        st("tensor", (lambda t, slot=slot, j=j, sub=sub, bk=bk,
                                      k=k: t.matmul(
                            pF[:, bk * 512: bk * 512 + 384],
                            w1s[:, slot * 2048 + k * 256 + j * 128:
                                slot * 2048 + k * 256 + (j + 1) * 128],
                            xgT[:, k * CP + sub * 384: k * CP + (sub + 1) * 384],
                            start=(k == 0), stop=(k == 7))),
                           waits=w,
                           incs=[(sS1, 1)] if (k == 7 and bk == 3) else [])
            s1_mark = sS1.count
            for j in range(2):
                f2c = f2cg * 2 + j
                for sub in range(2):
                    bk = j * 2 + sub
                    st("scalar", (lambda s, f2c=f2c, sub=sub, bk=bk:
                                  s.activation(
                        geluT[:, f2c * CP + sub * 384: f2c * CP + (sub + 1) * 384],
                        pF[:, bk * 512: bk * 512 + 384], AF.Gelu)),
                       waits=[(sS1, s1_mark), (sS2, sS2.count)],
                       incs=[(sGl, 1)])
        gl_mark = sGl.count

        for half in range(2):
            slot = w2ld[0] % 2
            w = []
            if w2ld[0] >= 2:
                w.append((sS2, sS2.count - MT))
            st("gpsimd", (lambda g, e=e, half=half, slot=slot: g.dma_start(
                out=w2s[:, slot * 8192:(slot + 1) * 8192].rearrange(
                    "p (c f) -> p c f", c=16),
                in_=din["w2r"][e, half])),
               waits=w, incs=[(sW2, 16)])
            w2ld[0] += 1
            ld2mark = sW2.count
            for m in range(MT):
                pp = m % 2
                for f2c in range(16):
                    w = []
                    if f2c == 0:
                        w = [(sW2, ld2mark), (sGl, gl_mark)]
                        if not (m < 2 and half == 0 and e == 0):
                            w.append((sHe, sHe.count - 1))
                    st("tensor", (lambda t, pp=pp, f2c=f2c, m=m, slot=slot:
                                  t.matmul(
                        pF[:, pp * 512: (pp + 1) * 512],
                        geluT[:, f2c * CP + m * 128: f2c * CP + (m + 1) * 128],
                        w2s[:, slot * 8192 + f2c * 512:
                            slot * 8192 + (f2c + 1) * 512],
                        start=(f2c == 0), stop=(f2c == 15))),
                       waits=w, incs=[(sS2, 1)] if f2c == 15 else [])
                s2m = sS2.count
                hslot = (sHe.count) % 2
                w = [(sS2, s2m)]
                if sHe.count >= 2:
                    w.append((sHs, (sHe.count - 1) * 16))
                st("vector", (lambda v, pp=pp, hslot=hslot, half=half:
                              v.tensor_copy(
                    hbuf[:, hslot * 1024 + half * 512:
                         hslot * 1024 + half * 512 + 512],
                    pF[:, pp * 512: (pp + 1) * 512])),
                   waits=w, incs=[(sHe, 1)])
                st("sync", (lambda s, e=e, m=m, hslot=hslot, half=half:
                            s.dma_start(
                    out=hdd[e * CP + m * 128: e * CP + (m + 1) * 128,
                            half * 512: half * 512 + 512],
                    in_=hbuf[:, hslot * 1024 + half * 512:
                             hslot * 1024 + half * 512 + 512])),
                   waits=[(sHe, None)], incs=[(sHs, 16)])
    ffn_done = sHs.count
    if hd.get("dbg_sp"):
        st("sync", (lambda s: s.dma_start(out=hd["dbg_sp"][2],
                                          in_=hdd[0:256, :])),
           waits=[(sHs, ffn_done)], incs=[(hd["sDbg"], 16)])

    for gidx in range(32):
        gslot = gidx % 4
        w = [(sHs, ffn_done), (sCast, 1)]
        if gidx >= 4:
            w.append((sLN2, (gidx - 3) * 7))
        st("gpsimd", (lambda g, gidx=gidx, gslot=gslot: g.indirect_dma_start(
            hgs[:, gslot * 1024:(gslot + 1) * 1024], None,
            hdd[0:128, :],
            IndirectOffsetOnAxis(dstint[:, gidx:gidx + 1], 0))),
           waits=w, incs=[(sHg, 16)])
        st("vector", (lambda v, gidx=gidx, gslot=gslot: v.scalar_tensor_tensor(
            z2b[:], hgs[:, gslot * 1024:(gslot + 1) * 1024],
            wvals[:, gidx:gidx + 1], x1[:, gidx * 1024:(gidx + 1) * 1024],
            ALU.mult, ALU.add)),
           waits=[(sHg, 16 * (gidx + 1)), (sRt, rt_done), (sSQ2, sSQ2.count)],
           incs=[(sLN2, 1)])
        st("vector", (lambda v: v.reduce_sum(s1[:, 0:1], z2b[:], AX.X)),
           waits=[(sLN2, None)], incs=[(sLN2, 1)])
        st("scalar", (lambda s: s.activation(zt[:], z2b[:], AF.Square,
                                             accum_out=s2[:, 0:1])),
           waits=[(sLN2, sLN2.count), (sSQ2, sSQ2.count)], incs=[(sSQ2, 1)])
        st("vector", (lambda v: v.tensor_scalar_mul(m1[:, 0:1], s1[:, 0:1],
                                                    1.0 / D)),
           waits=[(sLN2, None)], incs=[(sLN2, 1)])
        st("vector", (lambda v: v.tensor_scalar(msq[:, 0:1], m1[:, 0:1],
                                                m1[:, 0:1], None, ALU.mult)),
           waits=[(sLN2, None)], incs=[(sLN2, 1)])
        st("vector", (lambda v: v.scalar_tensor_tensor(
            vr[:, 0:1], s2[:, 0:1], 1.0 / D, msq[:, 0:1], ALU.mult,
            ALU.subtract)),
           waits=[(sSQ2, None), (sLN2, None)], incs=[(sLN2, 1)])
        st("scalar", (lambda s: s.activation(rstd[:, 0:1], vr[:, 0:1], AF.Sqrt,
                                             bias=C["epsb"][:, 0:1], scale=1.0)),
           waits=[(sLN2, None)], incs=[(sSQ2, 1)])
        st("vector", (lambda v: v.reciprocal(rstd[:, 0:1], rstd[:, 0:1])),
           waits=[(sSQ2, None)], incs=[(sLN2, 1)])
        pp = gidx % 2
        st("vector", (lambda v, pp=pp: v.tensor_scalar(
            outt[:, pp * 1024:(pp + 1) * 1024], z2b[:], m1[:, 0:1],
            rstd[:, 0:1], ALU.subtract, ALU.mult)),
           waits=[(sLN2, None), (sOut, max(0, (gidx - 1) * 16))],
           incs=[(sLN2, 1)])
        b_, row = gidx // 16, (gidx % 16) * 128
        st("sync", (lambda s, b_=b_, row=row, pp=pp: s.dma_start(
            out=y[b_, row:row + 128, :], in_=outt[:, pp * 1024:(pp + 1) * 1024])),
           waits=[(sLN2, None)], incs=[(sOut, 16)])
    st("sync", None, waits=[(sOut, 32 * 16)])
    st("gpsimd", None, waits=[(sOut, 32 * 16)])
    st("vector", None, waits=[(sOut, 32 * 16)])
    return hd

def build_full(fused=True, debug=False):
    nc = bass.Bass("TRN2", target_bir_lowering=False, debug=False)
    p = Prog(nc)
    din = declare_inputs(nc)
    hd = {}
    if debug:
        hd["dbg_rt"] = (
            nc.dram_tensor("d_th", [128, 8], F32, kind="ExternalOutput").ap(),
            nc.dram_tensor("d_ga", [128, 32 * 8], F32, kind="ExternalOutput").ap(),
            nc.dram_tensor("d_pg", [64, T], F32, kind="ExternalOutput").ap(),
        )
        hd["dbg_rt2"] = (
            nc.dram_tensor("d_gT", [8, T], F32, kind="ExternalOutput").ap(),
            nc.dram_tensor("d_poss", [8, T], F32, kind="ExternalOutput").ap(),
        )
        hd["dbg_sp"] = (
            nc.dram_tensor("d_dst", [128, 32], F32, kind="ExternalOutput").ap(),
            nc.dram_tensor("d_xg", [256, 1024], BF16, kind="ExternalOutput").ap(),
            nc.dram_tensor("d_hd", [256, 1024], BF16, kind="ExternalOutput").ap(),
        )
        hd["dbg_ffn"] = (
            nc.dram_tensor("d_gel", [128, 16 * 1024], BF16,
                           kind="ExternalOutput").ap(),
            nc.dram_tensor("d_z2", [128, 32 * 1024], BF16,
                           kind="ExternalOutput").ap(),
        )
    hd["consts"] = dict(
        idn_bf=p.sbuf("c_idnbf", [128, 128], BF16),
        bq=p.sbuf("c_bq", [128, 24], F32),
        ones_f32=p.sbuf("c_ones", [1, 128], F32),
        bout_bc=p.sbuf("c_boutbc", [128, 1024], BF16),
        expb=p.sbuf("c_expb", [128, 1], F32),
        epsb=p.sbuf("c_epsb", [128, 1], F32),
        ones32=p.sbuf("c_ones32", [128, 64], BF16),
        ones1b=p.sbuf("c_ones1b", [1, 128], BF16),
    )
    hd["x1"] = p.sbuf("x1", [128, 32 * 1024], BF16)
    hd["boutstage"] = hd["x1"]

    if debug:
        hd["sDbg"] = p.sem("sDbg")
    emit_attention(p, din, hd)

    p.open_phase()
    hd["probs"] = p.sbuf("probs", [128, 32 * 8], F32)
    hd["logits"] = p.sbuf("logits", [128, 32 * 8], F32)
    hd["gates"] = p.sbuf("gates", [128, 32 * 8], F32)
    hd["th8"] = p.sbuf("th8", [128, 8], F32)
    hd["idn_f32"] = p.sbuf("idn_f32", [128, 128], F32)
    hd["wg_sb"] = p.sbuf("wg_sb", [128, 64], BF16)

    hd["wvals"] = p.sbuf("wvals", [128, 32], F32)
    hd["dsti"] = p.sbuf("dsti", [128, 32], F32)
    p.open_phase()
    hd["ccsb"] = p.sbuf("ccsb", [8, T], F32)
    p.open_phase()
    emit_gating(p, din, hd)
    p.close_phase()
    emit_routing(p, din, hd, fused=fused)
    p.close_phase()

    emit_ffn_sparse(p, din, hd)
    p.close_phase()

    p.build_block()
    p.stack.close()
    return nc


def host_inmaps(inputs):
    Smat = np.zeros((64, 8), np.float32)
    for q in range(64):
        Smat[q, q % 8] = 1.0
    base = {
        "wqkv_r": np.ascontiguousarray(
            np.asarray(inputs["L0_attn_W_qkv"], np.float32)
            .reshape(8, 128, 24, 128).transpose(2, 1, 0, 3)),
        "bqkv_pc": np.ascontiguousarray(
            np.asarray(inputs["L0_attn_b_qkv"], np.float32).reshape(24, 128).T),
        "wout_r": np.ascontiguousarray(
            np.asarray(inputs["L0_attn_W_out"], np.float32)
            .reshape(8, 128, 1024).transpose(1, 0, 2)),
        "bout": np.ascontiguousarray(np.asarray(inputs["L0_attn_b_out"], np.float32)),
        "wg_r": np.ascontiguousarray(
            np.asarray(inputs["L0_ffn_Wg"], np.float32)
            .reshape(8, 128, 8).transpose(1, 0, 2)),
        "w1r": np.ascontiguousarray(
            np.asarray(inputs["L0_ffn_W1"], np.float32)
            .reshape(E, 8, 128, 8, 256).transpose(0, 3, 2, 1, 4)),
        "w2r": np.ascontiguousarray(
            np.asarray(inputs["L0_ffn_W2"], np.float32)
            .reshape(E, 16, 128, 2, 512).transpose(0, 3, 2, 1, 4)),
        "idn_bf": np.eye(128).astype(BFD),
        "idn_f32": np.eye(128, dtype=np.float32),
        "ones_f32": np.ones((1, 128), np.float32),
        "smat": Smat,
        "s2mat": np.ascontiguousarray(Smat.T),
        "e768": np.tile(np.arange(8, dtype=np.float32) * 768.0, 32).reshape(1, 256),
    }
    x = np.asarray(inputs["L0_attn_x"], np.float32)
    maps = []
    for c in range(NCORES):
        m = dict(base)
        m["x"] = np.ascontiguousarray(x[c * B:(c + 1) * B])
        maps.append(m)
    return maps


_NC_CACHE = {}


def kernel(**inputs):
    if "nc" not in _NC_CACHE:
        _NC_CACHE["nc"] = build_full(fused=True)
    nc = _NC_CACHE["nc"]
    maps = host_inmaps(inputs)
    res = run_bass_kernel_spmd(nc, maps, core_ids=list(range(NCORES)))
    return np.concatenate([np.asarray(r["y"]) for r in res.results], axis=0)


# version 12
# speedup vs baseline: 2.1568x; 1.0293x over previous
import math
from contextlib import ExitStack

import numpy as np
import ml_dtypes

import concourse.bass as bass
import concourse.mybir as mybir
from concourse.bass_utils import run_bass_kernel_spmd
from concourse.bass_core_rust import IndirectOffsetOnAxis

F32 = mybir.dt.float32
BF16 = mybir.dt.bfloat16
AF = mybir.ActivationFunctionType
ALU = mybir.AluOpType
AX = mybir.AxisListType

B = 2
S = 2048
D = 1024
H = 16
DH = 64
T = B * S
E = 8
FF = 2048
NQ = 4
NCORES = 8
BFD = ml_dtypes.bfloat16


class _SemW:
    __slots__ = ("h", "count")

    def __init__(self, h):
        self.h = h
        self.count = 0


class Prog:
    def __init__(self, nc):
        self.nc = nc
        self.steps = []
        self.stack = ExitStack()
        self.phases = []
        self._semi = 0
        self._wm = {}

    def open_phase(self):
        self.phases.append(ExitStack())

    def close_phase(self):
        self.phases.pop().close()

    def _alloc(self, guard):
        tgt = self.phases[-1] if self.phases else self.stack
        return tgt.enter_context(guard)

    def sem(self, name=None):
        h = self.stack.enter_context(self.nc.semaphore(name or f"s{self._semi}"))
        self._semi += 1
        return _SemW(h)

    def _uname(self, name):
        self._semi += 1
        return f"{name}_{self._semi}"

    def sbuf(self, name, shape, dtype):
        return self._alloc(self.nc.sbuf_tensor(self._uname(name), shape, dtype))

    def psum(self, name, shape, dtype=F32):
        return self._alloc(self.nc.psum_tensor(self._uname(name), shape, dtype))

    def step(self, eng, fn, waits=(), incs=()):
        w = []
        for (s, v) in waits:
            v = s.count if v is None else v
            key = (eng, id(s))
            if v > self._wm.get(key, 0):
                self._wm[key] = v
                w.append((s, v))
        for (s, n) in incs:
            s.count += n
        self.steps.append((eng, fn, w, list(incs)))

    def build_block(self):
        nc, steps = self.nc, self.steps

        def body(name):
            def run(eng):
                for (e, fn, waits, incs) in steps:
                    if e != name:
                        continue
                    for (s, v) in waits:
                        eng.wait_ge(s.h, v)
                    if fn is None:
                        continue
                    inst = fn(eng)
                    for (s, n) in incs:
                        inst = inst.then_inc(s.h, n)
            return run

        with nc.allow_low_precision(reason="bf16 activations, fp32 accumulation"):
            with nc.Block() as block:
                block.tensor(body("tensor"))
                block.vector(body("vector"))
                block.scalar(body("scalar"))
                block.sync(body("sync"))
                block.gpsimd(body("gpsimd"))


def declare_inputs(nc):
    d = {}
    d["x"] = nc.dram_tensor("x", [B, S, D], F32, kind="ExternalInput").ap()
    d["wqkv_r"] = nc.dram_tensor("wqkv_r", [24, 128, 8, 128], F32,
                                 kind="ExternalInput").ap()
    d["bqkv_pc"] = nc.dram_tensor("bqkv_pc", [128, 24], F32, kind="ExternalInput").ap()
    d["wout_r"] = nc.dram_tensor("wout_r", [128, 8, 1024], F32,
                                 kind="ExternalInput").ap()
    d["bout"] = nc.dram_tensor("bout", [D], F32, kind="ExternalInput").ap()
    d["wg_r"] = nc.dram_tensor("wg_r", [128, 8, 8], F32, kind="ExternalInput").ap()
    d["w1r"] = nc.dram_tensor("w1r", [E, 8, 128, 8, 256], F32,
                              kind="ExternalInput").ap()
    d["w2r"] = nc.dram_tensor("w2r", [E, 2, 128, 16, 512], F32,
                              kind="ExternalInput").ap()
    d["idn_bf"] = nc.dram_tensor("idn_bf", [128, 128], BF16, kind="ExternalInput").ap()
    d["idn_f32"] = nc.dram_tensor("idn_f32", [128, 128], F32, kind="ExternalInput").ap()
    d["ones_f32"] = nc.dram_tensor("ones_f32", [1, 128], F32, kind="ExternalInput").ap()
    d["smat"] = nc.dram_tensor("smat", [64, 8], F32, kind="ExternalInput").ap()
    d["s2mat"] = nc.dram_tensor("s2mat", [8, 64], F32, kind="ExternalInput").ap()
    d["e768"] = nc.dram_tensor("e768", [1, 256], F32, kind="ExternalInput").ap()
    return d


def emit_attention(p, din, hd):
    st = p.step
    C = hd["consts"]
    x1 = hd["x1"]

    ld = p.sem("ld")
    gld = p.sem("gld")
    sxl = p.sem("sxl")
    sTr = p.sem("sTr")
    sTrE = p.sem("sTrE")
    sWc = p.sem("sWc")
    sWcF = p.sem("sWcF")
    sQE = p.sem("sQE")
    sVt = p.sem("sVt")
    sVV = p.sem("sVV")
    sE = p.sem("sE")
    sXP = p.sem("sXP")
    sPV = p.sem("sPV")
    sUe = p.sem("sUe")
    sAT = p.sem("sAT")
    sOJ = p.sem("sOJ")
    sXt = p.sem("sXt")
    sXtF = p.sem("sXtF")
    sZ = p.sem("sZ")
    sZp = p.sem("sZp")
    sLN = p.sem("sLN")
    sSQ = p.sem("sSQ")

    st("sync", lambda s: s.dma_start(out=C["idn_bf"][:], in_=din["idn_bf"][:]),
       incs=[(ld, 16)])
    st("sync", lambda s: s.dma_start(out=C["bq"][:], in_=din["bqkv_pc"][:]),
       incs=[(ld, 16)])
    st("sync", lambda s: s.dma_start(out=C["ones_f32"][:], in_=din["ones_f32"][:]),
       incs=[(ld, 16)])
    st("gpsimd", lambda g: g.dma_start(
        out=hd["boutstage"][0:1, 0:1024],
        in_=din["bout"].rearrange("(a d) -> a d", a=1)),
       incs=[(gld, 16)])
    st("vector", lambda v: v.memset(C["ones1b"][:], 1.0), incs=[(sLN, 1)])
    st("vector", lambda v: v.memset(C["expb"][:], -40.0), incs=[(sLN, 1)])
    st("vector", lambda v: v.memset(C["epsb"][:], 1e-5), incs=[(sLN, 1)])
    st("vector", lambda v: v.memset(C["ones32"][:], 1.0 / 32.0), incs=[(sLN, 1)])

    p.open_phase()
    qT = p.sbuf("qT", [128, 8 * 2048], BF16)
    kT = p.sbuf("kT", [128, 8 * 2048], BF16)
    vv = p.sbuf("vv", [128, 16 * (16 * 65)], BF16)

    pA = p.psum("pA", [128, 1024], F32)
    pA2 = p.psum("pA2", [128, 1024], F32)

    st("tensor", lambda t: t.matmul(pA[:, 0:512], C["ones1b"][0:1, :],
                                    hd["boutstage"][0:1, 0:512],
                                    start=True, stop=True),
       waits=[(ld, None), (gld, None), (sLN, None)])
    st("tensor", lambda t: t.matmul(pA[:, 512:1024], C["ones1b"][0:1, :],
                                    hd["boutstage"][0:1, 512:1024],
                                    start=True, stop=True),
       incs=[(sE, 1)])
    st("vector", lambda v: v.tensor_copy(C["bout_bc"][:], pA[:, 0:1024]),
       waits=[(sE, None)], incs=[(sLN, 1)])
    cr = dict(ln=sLN.count, ld=ld.count, gld=gld.count)

    oj_mark = [0]
    z_mark = [0]
    zp_mark = [0]
    vr_mark = [0]

    for b in range(B):
        p.open_phase()
        pTr = p.psum("pTr", [128, 2048], BF16)
        xT = p.sbuf("xT", [128, 8 * 2048], BF16)
        xstage = p.sbuf("xstage", [128, 2 * 1024], BF16)
        NWS = 2
        wcol = p.sbuf("wcol", [128, NWS * 1024], BF16)
        vtmp = p.sbuf("vtmp", [128, 2 * 512], BF16)

        ntr = sTr.count
        for i in range(16):
            slot = i % 2
            if i >= 2:
                w = [(sTr, ntr + (i - 1) * 8)]
            elif b > 0:
                w = [(sTr, ntr), (sZ, sZ.count), (sXtF, sXtF.count),
                     (sSQ, sSQ.count), (sPV, sPV.count), (sOJ, sOJ.count)]
            else:
                w = []
            st("gpsimd", (lambda g, b=b, i=i, slot=slot: g.dma_start(
                out=xstage[:, slot * 1024:(slot + 1) * 1024],
                in_=din["x"][b, i * 128:(i + 1) * 128, :])),
               waits=w, incs=[(sxl, 16)])
            for c in range(8):
                j = i * 8 + c
                pp = (ntr + j) % 2
                w = [(sxl, sxl.count), (ld, cr["ld"])]
                if j >= 2:
                    w.append((sTrE, sTrE.count - 1))
                st("tensor", (lambda t, c=c, slot=slot, pp=pp: t.transpose(
                    pTr[:, pp * 1024: pp * 1024 + 128],
                    xstage[:, slot * 1024 + c * 128: slot * 1024 + (c + 1) * 128],
                    C["idn_bf"][:])),
                   waits=w, incs=[(sTr, 1)])
                st("vector", (lambda v, i=i, c=c, pp=pp: v.tensor_copy(
                    xT[:, c * 2048 + i * 128: c * 2048 + (i + 1) * 128],
                    pTr[:, pp * 1024: pp * 1024 + 128])),
                   waits=[(sTr, None)], incs=[(sTrE, 1)])
        xT_ready = sTrE.count

        st("vector", lambda v: v.memset(vv[:], 1.0),
           waits=[(sPV, sPV.count)], incs=[(sVV, 1)])
        vv_ones = sVV.count

        nwc0, nqe0pre = sWc.count, sQE.count
        for fout in range(24):
            slot = fout % NWS
            w = ([(sQE, nqe0pre + 4 * (fout - NWS + 1))] if fout >= NWS else [])
            st("gpsimd", (lambda g, fout=fout, slot=slot: g.dma_start(
                out=wcol[:, slot * 1024:(slot + 1) * 1024].rearrange(
                    "p (k c) -> p k c", k=8),
                in_=din["wqkv_r"][fout])),
               waits=w, incs=[(sWc, 16)])

        nqe0 = sQE.count
        for fout in range(24):
            slot = fout % NWS
            for pair in range(2):
                ps = pA if pair == 0 else pA2
                unit = fout * 2 + pair
                for k in range(8):
                    for nn in range(2):
                        n = pair * 2 + nn
                        w = []
                        if k == 0 and nn == 0:
                            w = [(sWc, nwc0 + (fout + 1) * 16), (sTrE, xT_ready)]
                            if unit >= 2:
                                w.append((sQE, nqe0 + 2 * (unit - 1)))
                            if unit < 2:
                                w.append((sLN, cr["ln"]))
                                w.append((sZp, zp_mark[0]))
                        inc = [(sE, 1)] if (k == 7 and nn == 1) else []
                        st("tensor", (lambda t, ps=ps, nn=nn, slot=slot, k=k, n=n:
                                      t.matmul(
                            ps[:, nn * 512:(nn + 1) * 512],
                            wcol[:, slot * 1024 + k * 128: slot * 1024 + (k + 1) * 128],
                            xT[:, k * 2048 + n * 512: k * 2048 + (n + 1) * 512],
                            start=(k == 0), stop=(k == 7))),
                           waits=w, incs=inc)
                mmk = sE.count
                for nn in range(2):
                    n = pair * 2 + nn
                    if fout < 8:
                        dstf = (lambda fout=fout, n=n:
                                qT[:, fout * 2048 + n * 512: fout * 2048 + (n + 1) * 512])
                    elif fout < 16:
                        dstf = (lambda fout=fout, n=n:
                                kT[:, (fout - 8) * 2048 + n * 512:
                                   (fout - 8) * 2048 + (n + 1) * 512])
                    else:
                        dstf = (lambda nn=nn: vtmp[:, nn * 512:(nn + 1) * 512])
                    w = [(sE, mmk), (ld, cr["ld"])]
                    if fout >= 16:
                        w.append((sTr, sTr.count))
                    incs = [(sQE, 1)]
                    st("scalar", (lambda s, ps=ps, nn=nn, dstf=dstf, fout=fout:
                                  s.activation(dstf(), ps[:, nn * 512:(nn + 1) * 512],
                                               AF.Identity,
                                               bias=C["bq"][:, fout:fout + 1],
                                               scale=1.0)),
                       waits=w, incs=incs)
                if fout >= 16:
                    fc = fout - 16
                    for nn in range(2):
                        n = pair * 2 + nn
                        for tsub in range(4):
                            kt = n * 4 + tsub
                            pp = sTr.count % 2
                            w = [(sQE, sQE.count), (sTrE, sTrE.count - 1)]
                            st("tensor", (lambda t, nn=nn, tsub=tsub, pp=pp: t.transpose(
                                pTr[:, pp * 1024: pp * 1024 + 128],
                                vtmp[:, nn * 512 + tsub * 128: nn * 512 + (tsub + 1) * 128],
                                C["idn_bf"][:])),
                               waits=w, incs=[(sTr, 1)])
                            for hh in range(2):
                                h = 2 * fc + hh
                                incs = [(sVV, 1)] if hh == 0 else [(sTrE, 1)]
                                st("vector", (lambda v, kt=kt, h=h, hh=hh, pp=pp:
                                              v.tensor_copy(
                                    vv[:, kt * 1040 + h * 65: kt * 1040 + h * 65 + 64],
                                    pTr[:, pp * 1024 + hh * 64: pp * 1024 + hh * 64 + 64])),
                                   waits=[(sTr, None), (sVV, vv_ones)], incs=incs)
        qkv_done = dict(qe=sQE.count, vv=sVV.count, tre=sTrE.count)
        if b == 0 and hd.get("dbg_qkv"):
            dq, dk, dv = hd["dbg_qkv"]
            sDbg = hd["sDbg"]
            st("sync", (lambda s, dq=dq: s.dma_start(out=dq, in_=qT[:])),
               waits=[(sQE, qkv_done["qe"]), (sTrE, qkv_done["tre"]),
                      (sVV, qkv_done["vv"])], incs=[(sDbg, 16)])
            st("sync", (lambda s, dk=dk: s.dma_start(out=dk, in_=kT[:])),
               incs=[(sDbg, 16)])
            st("sync", (lambda s, dv=dv: s.dma_start(out=dv, in_=vv[:])),
               incs=[(sDbg, 16)])
        p.close_phase()

        p.open_phase()
        pUs = [p.psum("pU0", [128, 512], F32), p.psum("pU1", [128, 512], F32)]
        pC = p.psum("pC", [128, 512], F32)
        ptile = p.sbuf("ptile", [128, 3 * 1024], BF16)
        attnT = p.sbuf("attnT", [128, 8 * 512], BF16)
        rrb = p.sbuf("rrb", [128, 512], BF16)
        rb_sb = p.sbuf("rb_sb", [128, 512], BF16)
        u2 = p.sbuf("u2", [128, 512], BF16)
        wout_sb = p.sbuf("wout_sb", [128, 8 * 1024], BF16)
        xtile = p.sbuf("xtile", [128, 2 * 1024], BF16)
        zbuf = p.sbuf("zbuf", [128, 1024], F32)
        s1 = p.sbuf("s1", [128, 1], F32)
        s2 = p.sbuf("s2", [128, 1], F32)
        m1 = p.sbuf("m1", [128, 1], F32)
        msq = p.sbuf("msq", [128, 1], F32)
        vr = p.sbuf("vr", [128, 1], F32)
        rstd = p.sbuf("rstd", [128, 1], F32)

        st("gpsimd", lambda g: g.dma_start(
            out=wout_sb[:].rearrange("p (k f) -> p k f", k=8),
            in_=din["wout_r"]),
           waits=[(sQE, qkv_done["qe"]), (sTrE, qkv_done["tre"]),
                  (sVV, qkv_done["vv"])],
           incs=[(gld, 16)])
        wout_ready = gld.count

        exp_marks = {}
        pv_marks = {}
        pu_free = {}
        tail_pvfin = {}
        tail_pending = [None]
        for n in range(NQ):
            for mblk in range(4):
                gidx = b * 16 + n * 4 + mblk
                slot = gidx % 2
                w = [(sXtF, gidx - 1), (sQE, qkv_done["qe"])]
                st("gpsimd", (lambda g, b=b, n=n, mblk=mblk, slot=slot: g.dma_start(
                    out=xtile[:, slot * 1024:(slot + 1) * 1024],
                    in_=din["x"][b, n * 512 + mblk * 128: n * 512 + (mblk + 1) * 128, :])),
                   waits=w, incs=[(sXt, 16)])

            for h in range(H):
                ch, hp = h // 2, h % 2
                pU = pUs[h % 2]
                e_marks = {}

                def emit_E(g):
                    pe = pA if g % 2 == 0 else pA2
                    for j in range(2):
                        kt = 2 * g + j
                        w = []
                        if j == 0:
                            w = [(sQE, qkv_done["qe"]), (sTrE, qkv_done["tre"]),
                                 (sVV, qkv_done["vv"])]
                            if (h, g - 2) in exp_marks:
                                w.append((sXP, exp_marks[(h, g - 2)]))
                            elif g < 2 and h > 0:
                                w.append((sXP, exp_marks[(h - 1, 6 + g)]))
                            elif g < 2 and h == 0:
                                w.append((sZp, zp_mark[0]))
                        st("tensor", (lambda t, pe=pe, j=j, kt=kt, ch=ch,
                                      hp=hp, n=n: t.matmul(
                            pe[:, j * 512:(j + 1) * 512],
                            kT[hp * 64:hp * 64 + 64,
                               ch * 2048 + kt * 128: ch * 2048 + (kt + 1) * 128],
                            qT[hp * 64:hp * 64 + 64,
                               ch * 2048 + n * 512: ch * 2048 + (n + 1) * 512],
                            start=True, stop=True)),
                           waits=w, incs=[(sE, 1)] if j == 1 else [])
                    e_marks[g] = sE.count

                def emit_exp(g):
                    pe = pA if g % 2 == 0 else pA2
                    pp = g % 3
                    w = [(sE, e_marks[g]), (sLN, cr["ln"])]
                    if (h, g - 3) in pv_marks:
                        w.append((sPV, pv_marks[(h, g - 3)]))
                    elif g < 3 and h > 0:
                        w.append((sPV, pv_marks[(h - 1, (6, 7, 5)[g])]))
                    st("scalar", (lambda s, pe=pe, pp=pp: s.activation(
                        ptile[:, pp * 1024:(pp + 1) * 1024], pe[:, 0:1024],
                        AF.Exp, bias=C["expb"][:, 0:1], scale=1.0)),
                       waits=w, incs=[(sXP, 1)])
                    exp_marks[(h, g)] = sXP.count

                def emit_PV(g):
                    pp = g % 3
                    for j in range(2):
                        kt = 2 * g + j
                        w = [(sXP, exp_marks[(h, g)])]
                        if g == 0 and j == 0 and (h - 2, b, n) in pu_free:
                            w.append(pu_free[(h - 2, b, n)])
                        inc = [(sPV, 1)] if j == 1 else []
                        st("tensor", (lambda t, j=j, kt=kt, h=h, pp=pp, g=g,
                                      pU=pU: t.matmul(
                            pU[0:65, :],
                            vv[:, kt * 1040 + h * 65: kt * 1040 + h * 65 + 65],
                            ptile[:, pp * 1024 + j * 512: pp * 1024 + (j + 1) * 512],
                            start=(g == 0 and j == 0), stop=(g == 7 and j == 1))),
                           waits=w, incs=inc)
                    pv_marks[(h, g)] = sPV.count

                def emit_tail(hh_, pU_):
                    ch_, hp_ = hh_ // 2, hh_ % 2
                    pvf = tail_pvfin[hh_]
                    st("vector", (lambda v, pU=pU_: v.reciprocal(
                        rrb[64:65, :], pU[64:65, :])),
                       waits=[(sPV, pvf)], incs=[(sUe, 1)])
                    st("tensor", (lambda t: t.matmul(
                        pC[0:64, :], C["ones32"][64:65, 0:64], rrb[64:65, :],
                        start=True, stop=True)),
                       waits=[(sUe, None), (sAT, sAT.count)], incs=[(sE, 1)])
                    st("vector", (lambda v: v.tensor_copy(rb_sb[0:64, :],
                                                          pC[0:64, :])),
                       waits=[(sE, None)], incs=[(sUe, 1)])
                    if hp_ == 0:
                        st("vector", (lambda v, ch=ch_, pU=pU_: v.tensor_tensor(
                            attnT[0:64, ch * 512:(ch + 1) * 512],
                            pU[0:64, :], rb_sb[0:64, :], ALU.mult)),
                           waits=[(sUe, None), (sOJ, oj_mark[0])],
                           incs=[(sAT, 1)])
                        pu_free[(hh_, b, n)] = (sAT, sAT.count)
                    else:
                        st("vector", (lambda v, pU=pU_: v.tensor_tensor(
                            u2[0:64, :], pU[0:64, :], rb_sb[0:64, :], ALU.mult)),
                           waits=[(sUe, None)], incs=[(sUe, 1)])
                        pu_free[(hh_, b, n)] = (sUe, sUe.count)
                        st("tensor", (lambda t: t.matmul(
                            pC[64:128, :], C["idn_bf"][0:64, 0:64], u2[0:64, :],
                            start=True, stop=True, tile_position=(0, 64))),
                           waits=[(sUe, None)], incs=[(sE, 1)])
                        st("vector", (lambda v, ch=ch_: v.tensor_copy(
                            attnT[64:128, ch * 512:(ch + 1) * 512],
                            pC[64:128, :])),
                           waits=[(sE, None), (sOJ, oj_mark[0])],
                           incs=[(sAT, 1)])

                emit_E(0)
                emit_exp(0)
                emit_E(1)
                emit_exp(1)
                for g in range(8):
                    if g == 7 and tail_pending[0] is not None:
                        emit_tail(*tail_pending[0])
                        tail_pending[0] = None
                    emit_PV(g)
                    if g + 2 < 8:
                        emit_E(g + 2)
                        emit_exp(g + 2)
                tail_pvfin[h] = sPV.count
                tail_pending[0] = (h, pU)
                if h == H - 1:
                    emit_tail(*tail_pending[0])
                    tail_pending[0] = None
            at_done = sAT.count
            if b == 0 and n == 0 and hd.get("dbg_at"):
                st("sync", (lambda s: s.dma_start(out=hd["dbg_at"], in_=attnT[:])),
                   waits=[(sAT, at_done)], incs=[(hd["sDbg"], 16)])

            for mblk in range(4):
                po = pA if mblk % 2 == 0 else pA2
                gidx = b * 16 + n * 4 + mblk
                for k in range(8):
                    for fo in range(2):
                        w = []
                        if k == 0 and fo == 0:
                            w = [(sAT, at_done), (gld, wout_ready)]
                            w.append((sZp, zp_mark[0] if mblk < 2
                                      else sZp.count))
                        st("tensor", (lambda t, po=po, k=k, fo=fo, mblk=mblk: t.matmul(
                            po[:, fo * 512:(fo + 1) * 512],
                            attnT[:, k * 512 + mblk * 128: k * 512 + (mblk + 1) * 128],
                            wout_sb[:, k * 1024 + fo * 512: k * 1024 + (fo + 1) * 512],
                            start=(k == 0), stop=(k == 7))),
                           waits=w, incs=[(sOJ, 1)] if (k == 7 and fo == 1) else [])
                oj_done = sOJ.count
                slot = gidx % 2
                w = [(sXt, 16 * (gidx + 1)), (sLN, cr["ln"]), (sSQ, sSQ.count)]
                if hd.get("_zw"):
                    w.append(hd.pop("_zw"))
                st("vector", (lambda v, slot=slot: v.tensor_tensor(
                    zbuf[:], xtile[:, slot * 1024:(slot + 1) * 1024],
                    C["bout_bc"][:], ALU.add)),
                   waits=w, incs=[(sXtF, 1)])
                for half in range(2):
                    st("vector", (lambda v, po=po, half=half: v.tensor_tensor(
                        zbuf[:, half * 512:(half + 1) * 512],
                        zbuf[:, half * 512:(half + 1) * 512],
                        po[:, half * 512:(half + 1) * 512], ALU.add)),
                       waits=[(sOJ, oj_done)],
                       incs=[(sZp, 1)] if half == 1 else [])
                zmark = sZp.count
                if b == 0 and n == 0 and mblk == 0 and hd.get("dbg_z"):
                    st("sync", (lambda s: s.dma_start(out=hd["dbg_z"], in_=zbuf[:])),
                       waits=[(sZ, zmark)], incs=[(hd["sDbg"], 16)])
                    st("sync", (lambda s: s.dma_start(out=hd["dbg_zx"], in_=zx[:])),
                       incs=[(hd["sDbg"], 16)])
                    hd["_zw"] = (hd["sDbg"], hd["sDbg"].count)
                st("vector", (lambda v: v.reduce_sum(s1[:, 0:1], zbuf[:], AX.X)),
                   incs=[(sLN, 1)])
                st("scalar", (lambda s: s.activation(ptile[:, 0:1024], zbuf[:],
                                                     AF.Square,
                                                     accum_out=s2[:, 0:1])),
                   waits=[(sZp, zmark), (sXP, sXP.count), (sPV, sPV.count),
                          (sLN, vr_mark[0])],
                   incs=[(sSQ, 1)])
                st("vector", (lambda v: v.tensor_scalar_mul(m1[:, 0:1], s1[:, 0:1],
                                                            1.0 / D)),
                   waits=[(sLN, None)], incs=[(sLN, 1)])
                st("vector", (lambda v: v.tensor_scalar(msq[:, 0:1], m1[:, 0:1],
                                                        m1[:, 0:1], None, ALU.mult)),
                   waits=[(sLN, None)], incs=[(sLN, 1)])
                st("vector", (lambda v: v.scalar_tensor_tensor(
                    vr[:, 0:1], s2[:, 0:1], 1.0 / D, msq[:, 0:1],
                    ALU.mult, ALU.subtract)),
                   waits=[(sSQ, None), (sLN, None)], incs=[(sLN, 1)])
                vr_mark[0] = sLN.count
                st("scalar", (lambda s: s.activation(rstd[:, 0:1], vr[:, 0:1], AF.Sqrt,
                                                     bias=C["epsb"][:, 0:1], scale=1.0)),
                   waits=[(sLN, None)], incs=[(sSQ, 1)])
                st("vector", (lambda v: v.reciprocal(rstd[:, 0:1], rstd[:, 0:1])),
                   waits=[(sSQ, None)], incs=[(sLN, 1)])
                col = gidx * 1024
                st("vector", (lambda v, col=col: v.tensor_scalar(
                    x1[:, col:col + 1024], zbuf[:], m1[:, 0:1], rstd[:, 0:1],
                    ALU.subtract, ALU.mult)),
                   waits=[(sLN, None)], incs=[(sZ, 1)])
            oj_mark[0] = sOJ.count
            z_mark[0] = sZ.count
            zp_mark[0] = sZp.count
        p.close_phase()
    p.close_phase()

    hd["attn"] = dict(z=sZ.count, sZ=sZ, sLN=sLN, ln=sLN.count)
    return hd


def emit_gating(p, din, hd):
    st = p.step
    C = hd["consts"]
    x1 = hd["x1"]
    probs, logits, ccsb = hd["probs"], hd["logits"], hd["ccsb"]
    sZ, zdone = hd["attn"]["sZ"], hd["attn"]["z"]

    sTr = p.sem("gTr")
    sTrE = p.sem("gTrE")
    sLg = p.sem("gLg")
    sLgE = p.sem("gLgE")
    sPr = p.sem("gPr")
    sPs = p.sem("gPs")
    sCt = p.sem("gCt")
    sCe = p.sem("gCe")
    gld2 = p.sem("gld2")

    idn_f32 = hd["idn_f32"]
    wg_sb = hd["wg_sb"]
    x1tr = p.sbuf("x1tr", [128, 2 * 1024], BF16)
    eall = p.sbuf("eall", [128, 256], F32)
    sall = p.sbuf("sall", [128, 32], F32)
    rall = p.sbuf("rall", [128, 32], F32)

    st("sync", lambda s: s.dma_start(out=idn_f32[:], in_=din["idn_f32"][:]),
       waits=[(sZ, zdone)], incs=[(gld2, 16)])
    st("gpsimd", lambda g: g.dma_start(
        out=wg_sb[:].rearrange("p (k e) -> p k e", k=8),
        in_=din["wg_r"]),
       waits=[(sZ, zdone)], incs=[(gld2, 16)])

    pTr = p.psum("gpTr", [128, 2048], BF16)
    pL = p.psum("gpL", [128, 2048], F32)
    pT8 = p.psum("gpT8", [128, 1024], F32)

    for i in range(32):
        slot = i % 2
        for c in range(8):
            j = i * 8 + c
            pp = j % 2
            w = [(sZ, zdone)]
            if j >= 2:
                w.append((sTrE, sTrE.count - 1))
            if c == 0 and i >= 2:
                w.append((sLg, sLg.count - 1))
            st("tensor", (lambda t, i=i, c=c, pp=pp: t.transpose(
                pTr[:, pp * 1024: pp * 1024 + 128],
                x1[:, i * 1024 + c * 128: i * 1024 + (c + 1) * 128],
                C["idn_bf"][:])),
               waits=w, incs=[(sTr, 1)])
            st("vector", (lambda v, slot=slot, c=c, pp=pp: v.tensor_copy(
                x1tr[:, slot * 1024 + c * 128: slot * 1024 + (c + 1) * 128],
                pTr[:, pp * 1024: pp * 1024 + 128])),
               waits=[(sTr, None)], incs=[(sTrE, 1)])
        tr_mark = sTrE.count
        ppl = i % 4
        for k in range(8):
            w = []
            if k == 0:
                w = [(sTrE, tr_mark), (gld2, None)]
                if i >= 4:
                    w.append((sLgE, sLgE.count - 3))
            st("tensor", (lambda t, ppl=ppl, k=k, slot=slot: t.matmul(
                pL[:, ppl * 512: ppl * 512 + 8],
                x1tr[:, slot * 1024 + k * 128: slot * 1024 + (k + 1) * 128],
                wg_sb[:, k * 8:(k + 1) * 8],
                start=(k == 0), stop=(k == 7))),
               waits=w, incs=[(sLg, 1)] if k == 7 else [])
        st("vector", (lambda v, i=i, ppl=ppl: v.tensor_copy(
            logits[:, i * 8:(i + 1) * 8], pL[:, ppl * 512: ppl * 512 + 8])),
           waits=[(sLg, None)], incs=[(sLgE, 1)])

    st("scalar", (lambda s: s.activation(eall[:], logits[:], AF.Exp)),
       waits=[(sLgE, None)], incs=[(sPs, 1)])
    st("vector", (lambda v: v.tensor_reduce(
        sall[:], eall[:].rearrange("p (i e) -> p i e", e=8), AX.X, ALU.add)),
       waits=[(sPs, None)], incs=[(sPr, 1)])
    st("vector", (lambda v: v.reciprocal(rall[:], sall[:])),
       waits=[(sPr, None)], incs=[(sPr, 1)])
    for i in range(32):
        st("vector", (lambda v, i=i: v.tensor_scalar(
            probs[:, i * 8:(i + 1) * 8], eall[:, i * 8:(i + 1) * 8],
            rall[:, i:i + 1], None, ALU.mult)),
           waits=[(sPr, None)], incs=[(sPr, 1)])
    for i in range(32):
        ppt = i % 2
        st("tensor", (lambda t, i=i, ppt=ppt: t.transpose(
            pT8[0:8, ppt * 512: ppt * 512 + 128], probs[:, i * 8:(i + 1) * 8],
            idn_f32[:])),
           waits=[(sPr, None), (sCe, sCe.count - 1)], incs=[(sCt, 1)])
        st("vector", (lambda v, i=i, ppt=ppt: v.tensor_copy(
            ccsb[0:8, i * 128:(i + 1) * 128], pT8[0:8, ppt * 512: ppt * 512 + 128])),
           waits=[(sCt, None)], incs=[(sCe, 1)])

    hd["gating"] = dict(ce=sCe.count, pr=sPr.count, sCe=sCe, sPr=sPr)
    return hd


def emit_routing(p, din, hd, fused=True):
    nc = p.nc
    st = p.step
    C = hd["consts"]
    probs = hd["probs"]
    gates = hd["gates"]
    ce_done = hd["gating"]["ce"]
    sCe = hd["gating"]["sCe"]
    sPr2 = hd["gating"]["sPr"]
    NITER = 28

    sCC = p.sem("rCC")
    sGt = p.sem("rGt")
    sGm = p.sem("rGm")
    sDs = p.sem("rDs")
    sAG = p.sem("rAG")
    sPG = p.sem("rPG")
    sBi = p.sem("rBi")
    sBm = p.sem("rBm")
    sRt = p.sem("rRt")

    smat_sb = p.sbuf("smat", [64, 8], F32)
    s2mat_sb = p.sbuf("s2mat", [8, 64], F32)
    pg = p.sbuf("pg", [64, T], F32)
    trash64 = p.sbuf("trash64", [64, T], BF16)
    cnt = p.sbuf("cnt", [64, 1], F32)
    thv = p.sbuf("thv", [64, 1], F32)
    lo = [p.sbuf("lo_a", [8, 1], F32), p.sbuf("lo_b", [8, 1], F32)]
    hi = [p.sbuf("hi_a", [8, 1], F32), p.sbuf("hi_b", [8, 1], F32)]
    mid = p.sbuf("mid", [8, 1], F32)
    midt = p.sbuf("midt", [8, 1], F32)
    pred = p.sbuf("pred", [8, 1], F32)
    npred = p.sbuf("npred", [8, 1], F32)
    t1 = p.sbuf("t1", [8, 1], F32)
    t2 = p.sbuf("t2", [8, 1], F32)
    throw = p.sbuf("throw", [1, 8], F32)
    th8 = hd["th8"]
    mx_a = p.sbuf("mx_a", [128, 32], F32)
    wvals = hd["wvals"]
    dsti = hd["dsti"]
    gT = p.sbuf("gT", [8, T], F32)
    possT = p.sbuf("possT", [8, T], F32)
    zer8 = p.sbuf("zer8", [8, T], F32)
    postok_a = p.sbuf("postok_a", [128, 256], F32)
    e768_a = p.sbuf("e768_a", [128, 256], F32)
    clamp_a = p.sbuf("clamp_a", [128, 256], F32)
    selb_a = p.sbuf("selb_a", [128, 256], F32)
    score_a = p.sbuf("score_a", [128, 256], F32)
    oh_a = p.sbuf("oh_a", [128, 256], F32)

    pRt = p.psum("pRt", [128, 1024], F32)
    pOh = p.psum("pOh", [128, 1024], F32)

    st("sync", lambda s: s.dma_start(out=smat_sb[:], in_=din["smat"][:]),
       waits=[(sCe, ce_done), (sPr2, hd["gating"]["pr"])], incs=[(sCC, 16)])
    st("sync", lambda s: s.dma_start(out=s2mat_sb[:], in_=din["s2mat"][:]),
       incs=[(sCC, 16)])

    if fused:
        cc_in = nc.dram_tensor("cc_in", [8, T], F32, kind="Internal").ap()
        cc_out = nc.dram_tensor("cc_out", [64, T], F32, kind="Internal",
                                addr_space="Shared").ap()
        st("sync", lambda s: s.dma_start(out=cc_in[:, :], in_=hd["ccsb"][:]),
           waits=[(sCe, ce_done)], incs=[(sCC, 16)])
        st("gpsimd", lambda g: g.collective_compute(
            "AllGather", ALU.bypass, replica_groups=[[0, 1, 2, 3, 4, 5, 6, 7]],
            ins=[cc_in[:, :]], outs=[cc_out[:, :]]),
           waits=[(sCC, None)], incs=[(sAG, 1)])
        st("sync", lambda s: s.dma_start(out=pg[:], in_=cc_out[:, :]),
           waits=[(sAG, 1)], incs=[(sPG, 16)])
    else:
        probs_all = nc.dram_tensor("probs_all", [64, T], F32,
                                   kind="ExternalInput").ap()
        st("sync", lambda s: s.dma_start(out=pg[:], in_=probs_all[:]),
           waits=[(sCC, None)], incs=[(sPG, 16)])

    st("vector", lambda v: v.memset(lo[0][:], 0.0), incs=[(sBi, 1)])
    st("vector", lambda v: v.memset(hi[0][:], 1.0), incs=[(sBi, 1)])
    st("sync", lambda s: s.dma_start(out=e768_a[0:1, :], in_=din["e768"][0:1, :]),
       waits=[(sCe, ce_done)], incs=[(sCC, 16)])
    st("tensor", lambda t: t.matmul(pOh[0:128, 0:256], C["ones_f32"][0:1, :],
                                    e768_a[0:1, 0:256], start=True, stop=True),
       waits=[(sCC, None)], incs=[(sGm, 1)])
    st("vector", lambda v: v.tensor_copy(e768_a[:], pOh[0:128, 0:256]),
       waits=[(sGm, None)], incs=[(sDs, 1)])
    st("vector", lambda v: v.tensor_scalar(clamp_a[:], e768_a[:], 767.0, None,
                                           ALU.add),
       waits=[(sDs, None)], incs=[(sDs, 1)])

    for it in range(NITER):
        a, b2 = it % 2, (it + 1) % 2
        bank = it % 2
        st("vector", (lambda v, a=a: v.tensor_scalar(
            mid[:], lo[a][:], hi[a][0:8, 0:1], 0.5, ALU.add, ALU.mult)),
           waits=[(sBi, None)], incs=[(sBi, 1)])
        st("tensor", (lambda t, bank=bank: t.matmul(
            pRt[0:64, bank * 512: bank * 512 + 1], s2mat_sb[:], mid[:],
            start=True, stop=True)),
           waits=[(sBi, None), (sCC, None), (sCe, ce_done)], incs=[(sBm, 1)])
        st("vector", (lambda v, bank=bank: v.tensor_copy(
            thv[:], pRt[0:64, bank * 512: bank * 512 + 1])),
           waits=[(sBm, None)], incs=[(sBi, 1)])
        st("vector", (lambda v: v.tensor_scalar(
            trash64[:], pg[:], thv[0:64, 0:1], 0.0, ALU.is_ge, ALU.add,
            accum_out=cnt[0:64, 0:1])),
           waits=[(sBi, None), (sPG, None)], incs=[(sBi, 1)])
        st("tensor", (lambda t, bank=bank: t.matmul(
            pRt[0:8, bank * 512 + 256: bank * 512 + 257], smat_sb[:],
            cnt[0:64, 0:1], start=True, stop=True)),
           waits=[(sBi, None)], incs=[(sBm, 1)])
        st("vector", (lambda v, bank=bank: v.tensor_scalar(
            pred[:], pRt[0:8, bank * 512 + 256: bank * 512 + 257], 4096.0, None,
            ALU.is_gt)),
           waits=[(sBm, None)], incs=[(sBi, 1)])
        st("vector", (lambda v, a=a: v.tensor_tensor(t1[:], mid[:], lo[a][:],
                                                     ALU.subtract)),
           waits=[(sBi, None)], incs=[(sBi, 1)])
        st("vector", (lambda v, a=a, b2=b2: v.scalar_tensor_tensor(
            lo[b2][:], t1[:], pred[0:8, 0:1], lo[a][:], ALU.mult, ALU.add)),
           waits=[(sBi, None)], incs=[(sBi, 1)])
        st("vector", (lambda v, a=a: v.tensor_tensor(t2[:], hi[a][:], mid[:],
                                                     ALU.subtract)),
           waits=[(sBi, None)], incs=[(sBi, 1)])
        st("vector", (lambda v, b2=b2: v.scalar_tensor_tensor(
            hi[b2][:], t2[:], pred[0:8, 0:1], mid[:], ALU.mult, ALU.add)),
           waits=[(sBi, None)], incs=[(sBi, 1)])
    fin = NITER % 2
    st("tensor", (lambda t, fin=fin: t.matmul(
        pRt[0:1, 0:8], hi[fin][:], hd["idn_f32"][0:8, 0:8], start=True,
        stop=True)),
       waits=[(sBi, None)], incs=[(sBm, 1)])
    st("vector", lambda v: v.tensor_copy(throw[:], pRt[0:1, 0:8]),
       waits=[(sBm, None)], incs=[(sBi, 1)])
    st("tensor", lambda t: t.matmul(pRt[0:128, 512:520], C["ones_f32"][0:1, :],
                                    throw[:], start=True, stop=True),
       waits=[(sBi, None)], incs=[(sBm, 1)])
    st("vector", lambda v: v.tensor_copy(th8[:], pRt[0:128, 512:520]),
       waits=[(sBm, None)], incs=[(sBi, 1)])
    th_done = sBi.count

    def b8(ap_small):
        return ap_small[:].rearrange("p (i o) -> p i o", o=1).to_broadcast(
            [128, 32, 8])

    th8b = th8[:].rearrange("p (i o) -> p i o", o=8).to_broadcast([128, 32, 8])
    pr3 = probs[:].rearrange("p (i e) -> p i e", e=8)
    st("vector", (lambda v: v.tensor_tensor(
        selb_a[:].rearrange("p (i e) -> p i e", e=8), pr3, th8b, ALU.is_ge)),
       waits=[(sBi, th_done), (sPr2, hd["gating"]["pr"])], incs=[(sRt, 1)])
    st("vector", (lambda v: v.scalar_tensor_tensor(
        score_a[:], selb_a[:], 2.0, probs[:], ALU.mult, ALU.add)),
       waits=[(sRt, None)], incs=[(sRt, 1)])
    st("vector", (lambda v: v.tensor_reduce(
        mx_a[:], score_a[:].rearrange("p (i e) -> p i e", e=8), AX.X, ALU.max)),
       waits=[(sRt, None)], incs=[(sRt, 1)])
    st("vector", (lambda v: v.tensor_tensor(
        oh_a[:].rearrange("p (i e) -> p i e", e=8),
        score_a[:].rearrange("p (i e) -> p i e", e=8), b8(mx_a), ALU.is_equal)),
       waits=[(sRt, None)], incs=[(sRt, 1)])
    st("vector", (lambda v: v.tensor_tensor(
        score_a[:], probs[:], oh_a[:], ALU.mult)),
       waits=[(sRt, None)], incs=[(sRt, 1)])
    st("vector", (lambda v: v.tensor_reduce(
        wvals[:], score_a[:].rearrange("p (i e) -> p i e", e=8), AX.X, ALU.max)),
       waits=[(sRt, None)], incs=[(sRt, 1)])
    st("vector", (lambda v: v.tensor_tensor(
        gates[:].rearrange("p (i e) -> p i e", e=8),
        oh_a[:].rearrange("p (i e) -> p i e", e=8), b8(wvals), ALU.mult)),
       waits=[(sRt, None)], incs=[(sRt, 1)])
    gates_done = sRt.count
    for i in range(32):
        ppo = i % 2
        st("tensor", (lambda t, i=i, ppo=ppo: t.transpose(
            pOh[0:8, ppo * 512: ppo * 512 + 128], gates[:, i * 8:(i + 1) * 8],
            hd["idn_f32"][:])),
           waits=[(sRt, gates_done), (sGt, sGt.count - 1), (sDs, 2)],
           incs=[(sGm, 1)])
        st("vector", (lambda v, i=i, ppo=ppo: v.tensor_copy(
            gT[0:8, i * 128:(i + 1) * 128],
            pOh[0:8, ppo * 512: ppo * 512 + 128])),
           waits=[(sGm, None)], incs=[(sGt, 1)])
    gt_done = sGt.count

    st("vector", lambda v: v.memset(zer8[:], 0.0), incs=[(sBi, 1)])
    st("vector", (lambda v: v.tensor_scalar(possT[:], gT[:], 0.0, None,
                                            ALU.is_gt)),
       waits=[(sGt, gt_done), (sBi, None)], incs=[(sBi, 1)])
    st("vector", (lambda v: v.tensor_tensor_scan(
        possT[:], possT[:], zer8[:], 0.0, ALU.add, ALU.add)),
       waits=[(sBi, None)], incs=[(sBi, 1)])
    scan_done = sBi.count

    for i in range(32):
        ppo = i % 2
        st("tensor", (lambda t, i=i, ppo=ppo: t.transpose(
            pOh[0:128, ppo * 512: ppo * 512 + 8],
            possT[0:8, i * 128:(i + 1) * 128], hd["idn_f32"][0:8, 0:8])),
           waits=[(sBi, scan_done), (sDs, sDs.count - 1)], incs=[(sGm, 1)])
        st("vector", (lambda v, i=i, ppo=ppo: v.tensor_copy(
            postok_a[:, i * 8:(i + 1) * 8],
            pOh[0:128, ppo * 512: ppo * 512 + 8])),
           waits=[(sGm, None)], incs=[(sDs, 1)])
    pos_done = sDs.count
    st("vector", (lambda v: v.scalar_tensor_tensor(
        postok_a[:], postok_a[:], 1.0, e768_a[:], ALU.subtract, ALU.add)),
       waits=[(sDs, pos_done)], incs=[(sDs, 1)])
    st("vector", (lambda v: v.tensor_tensor(postok_a[:], postok_a[:],
                                            clamp_a[:], ALU.min)),
       waits=[(sDs, None)], incs=[(sDs, 1)])
    st("vector", (lambda v: v.tensor_tensor(postok_a[:], postok_a[:], oh_a[:],
                                            ALU.mult)),
       waits=[(sDs, None)], incs=[(sDs, 1)])
    st("vector", (lambda v: v.tensor_reduce(
        dsti[:], postok_a[:].rearrange("p (i e) -> p i e", e=8), AX.X,
        ALU.add)),
       waits=[(sDs, None)], incs=[(sDs, 1)])
    if hd.get("dbg_rt"):
        dth, dga, dpg = hd["dbg_rt"]
        st("sync", (lambda s: s.dma_start(out=dth, in_=th8[:])),
           waits=[(sBi, th_done)], incs=[(hd["sDbg"], 16)])
        st("sync", (lambda s: s.dma_start(out=dga, in_=gates[:])),
           waits=[(sRt, sRt.count)], incs=[(hd["sDbg"], 16)])
        st("sync", (lambda s: s.dma_start(out=dpg, in_=pg[:])),
           waits=[(sPG, None)], incs=[(hd["sDbg"], 16)])
        st("sync", None, waits=[(hd["sDbg"], hd["sDbg"].count)])
    if hd.get("dbg_rt2"):
        st("sync", (lambda s: s.dma_start(out=hd["dbg_rt2"][0], in_=gT[:])),
           waits=[(sGt, gt_done)], incs=[(hd["sDbg"], 16)])
        st("sync", (lambda s: s.dma_start(out=hd["dbg_rt2"][1], in_=possT[:])),
           waits=[(sDs, sDs.count)], incs=[(hd["sDbg"], 16)])
    hd["routing"] = dict(rt=sRt.count, sRt=sRt, sBi=sBi, bi=sBi.count, sPG=sPG,
                         sDs=sDs, ds=sDs.count)
    return hd


def emit_ffn_dense(p, din, hd):
    st = p.step
    C = hd["consts"]
    x1 = hd["x1"]
    gates = hd["gates"]
    sRt, rt_done = hd["routing"]["sRt"], hd["routing"]["rt"]
    sBi, bi_done = hd["routing"]["sBi"], hd["routing"]["bi"]
    sPG = hd["routing"]["sPG"]

    sW1 = p.sem("fW1")
    sW2 = p.sem("fW2")
    sTr3 = p.sem("fTr")
    sTrE3 = p.sem("fTrE")
    sS1 = p.sem("fS1")
    sGl = p.sem("fGl")
    sS2 = p.sem("fS2")
    sGE = p.sem("fGE")
    sLN2 = p.sem("fLN2")
    sSQ2 = p.sem("fSQ2")
    sOut = p.sem("fOut")

    x1Tg = p.sbuf("x1Tg", [128, 8 * 1024], BF16)
    w1s = p.sbuf("w1s", [128, 2 * 2048], BF16)
    w2s = p.sbuf("w2s", [128, 2 * 8192], BF16)
    geluT = p.sbuf("geluT", [128, 16 * 1024], BF16)
    outt = p.sbuf("outt", [128, 2 * 1024], F32)
    s1 = p.sbuf("f_s1", [128, 1], F32)
    s2 = p.sbuf("f_s2", [128, 1], F32)
    m1 = p.sbuf("f_m1", [128, 1], F32)
    msq = p.sbuf("f_msq", [128, 1], F32)
    vr = p.sbuf("f_vr", [128, 1], F32)
    rstd = p.sbuf("f_rstd", [128, 1], F32)
    trln = p.sbuf("f_trln", [128, 1024], BF16)

    pF = p.psum("pF", [128, 2048], F32)
    pTr3 = p.psum("fpTr", [128, 2048], BF16)

    y = p.nc.dram_tensor("y", [B, S, D], F32, kind="ExternalOutput").ap()

    w1ld = [0]
    w2ld = [0]
    for grp in range(4):
        ntr = sTr3.count
        for mm_ in range(8):
            gidx = grp * 8 + mm_
            for c in range(8):
                j = sTr3.count - ntr
                pp = sTr3.count % 2
                w = [(sRt, rt_done)]
                if j >= 2:
                    w.append((sTrE3, sTrE3.count - 1))
                if j == 0:
                    w.append((sS1, sS1.count))
                st("tensor", (lambda t, gidx=gidx, c=c, pp=pp: t.transpose(
                    pTr3[:, pp * 1024: pp * 1024 + 128],
                    x1[:, gidx * 1024 + c * 128: gidx * 1024 + (c + 1) * 128],
                    C["idn_bf"][:])),
                   waits=w, incs=[(sTr3, 1)])
                st("vector", (lambda v, mm_=mm_, c=c, pp=pp: v.tensor_copy(
                    x1Tg[:, c * 1024 + mm_ * 128: c * 1024 + (mm_ + 1) * 128],
                    pTr3[:, pp * 1024: pp * 1024 + 128])),
                   waits=[(sTr3, None)], incs=[(sTrE3, 1)])
        trg_done = sTrE3.count

        for e in range(E):
            for f2cg in range(8):
                slot = w1ld[0] % 2
                w = [(sBi, bi_done), (sPG, 16)]
                if w1ld[0] >= 2:
                    w.append((sS1, sS1.count - 1))
                st("gpsimd", (lambda g, e=e, f2cg=f2cg, slot=slot: g.dma_start(
                    out=w1s[:, slot * 2048:(slot + 1) * 2048].rearrange(
                        "p (k c) -> p k c", k=8),
                    in_=din["w1r"][e, f2cg])),
                   waits=w, incs=[(sW1, 16)])
                w1ld[0] += 1
                ldmark = sW1.count
                for k in range(8):
                    for j in range(2):
                        for sub in range(2):
                            bk = j * 2 + sub
                            w = []
                            if k == 0 and j == 0 and sub == 0:
                                w = [(sW1, ldmark), (sTrE3, trg_done),
                                     (sGE, sGE.count), (sGl, sGl.count)]
                            st("tensor", (lambda t, slot=slot, j=j, sub=sub,
                                          bk=bk, k=k: t.matmul(
                                pF[:, bk * 512:(bk + 1) * 512],
                                w1s[:, slot * 2048 + k * 256 + j * 128:
                                    slot * 2048 + k * 256 + (j + 1) * 128],
                                x1Tg[:, k * 1024 + sub * 512:
                                     k * 1024 + (sub + 1) * 512],
                                start=(k == 0), stop=(k == 7))),
                               waits=w,
                               incs=[(sS1, 1)] if (k == 7 and bk == 3) else [])
                s1_mark = sS1.count
                for j in range(2):
                    f2c = f2cg * 2 + j
                    st("scalar", (lambda s, j=j, f2c=f2c: s.activation(
                        geluT[:, f2c * 1024:(f2c + 1) * 1024],
                        pF[:, j * 1024:(j + 1) * 1024], AF.Gelu)),
                       waits=[(sS1, s1_mark)], incs=[(sGl, 1)])
            gl_mark = sGl.count
            if grp == 0 and e == 0 and hd.get("dbg_ffn"):
                st("sync", (lambda s: s.dma_start(out=hd["dbg_ffn"][0],
                                                  in_=geluT[:])),
                   waits=[(sGl, gl_mark)], incs=[(hd["sDbg"], 16)])
                hd["_fw"] = True
            for half in range(2):
                slot = w2ld[0] % 2
                w = []
                if hd.pop("_fw", False):
                    w.append((hd["sDbg"], hd["sDbg"].count))
                if w2ld[0] >= 2:
                    w.append((sS2, sS2.count - 7))
                st("gpsimd", (lambda g, e=e, half=half, slot=slot: g.dma_start(
                    out=w2s[:, slot * 8192:(slot + 1) * 8192].rearrange(
                        "p (c f) -> p c f", c=16),
                    in_=din["w2r"][e, half])),
                   waits=w, incs=[(sW2, 16)])
                w2ld[0] += 1
                ld2mark = sW2.count
                for m in range(8):
                    gidx = grp * 8 + m
                    pp = m % 2
                    for f2c in range(16):
                        w = []
                        if f2c == 0:
                            w = [(sW2, ld2mark), (sGl, gl_mark)]
                            if not (m < 2 and half == 0 and e == 0 and grp == 0):
                                w.append((sGE, sGE.count - 1))
                        st("tensor", (lambda t, pp=pp, f2c=f2c, m=m, half=half,
                                      slot=slot: t.matmul(
                            pF[:, pp * 512: (pp + 1) * 512],
                            geluT[:, f2c * 1024 + m * 128: f2c * 1024 + (m + 1) * 128],
                            w2s[:, slot * 8192 + f2c * 512:
                                slot * 8192 + (f2c + 1) * 512],
                            start=(f2c == 0), stop=(f2c == 15))),
                           waits=w, incs=[(sS2, 1)] if f2c == 15 else [])
                    s2m = sS2.count
                    st("vector", (lambda v, gidx=gidx, e=e, pp=pp, half=half:
                                  v.scalar_tensor_tensor(
                        x1[:, gidx * 1024 + half * 512:
                           gidx * 1024 + half * 512 + 512],
                        pF[:, pp * 512: (pp + 1) * 512],
                        gates[:, gidx * 8 + e: gidx * 8 + e + 1],
                        x1[:, gidx * 1024 + half * 512:
                           gidx * 1024 + half * 512 + 512],
                        ALU.mult, ALU.add)),
                       waits=[(sS2, s2m), (sRt, rt_done)], incs=[(sGE, 1)])
    ffn_done = sGE.count
    if hd.get("dbg_ffn"):
        st("sync", (lambda s: s.dma_start(out=hd["dbg_ffn"][1], in_=x1[:])),
           waits=[(sGE, ffn_done)], incs=[(hd["sDbg"], 16)])
        st("sync", None, waits=[(hd["sDbg"], hd["sDbg"].count)])

    for gidx in range(32):
        pp = gidx % 2
        w = [(sGE, ffn_done), (sSQ2, sSQ2.count)]
        if gidx >= 2:
            w.append((sOut, (gidx - 1) * 16))
        st("vector", (lambda v, gidx=gidx: v.reduce_sum(
            s1[:, 0:1], x1[:, gidx * 1024:(gidx + 1) * 1024], AX.X)),
           waits=w, incs=[(sLN2, 1)])
        st("scalar", (lambda s, gidx=gidx: s.activation(
            trln[:], x1[:, gidx * 1024:(gidx + 1) * 1024], AF.Square,
            accum_out=s2[:, 0:1])),
           waits=[(sGE, ffn_done), (sLN2, sLN2.count - 1)], incs=[(sSQ2, 1)])
        st("vector", (lambda v: v.tensor_scalar_mul(m1[:, 0:1], s1[:, 0:1],
                                                    1.0 / D)),
           waits=[(sLN2, None)], incs=[(sLN2, 1)])
        st("vector", (lambda v: v.tensor_scalar(msq[:, 0:1], m1[:, 0:1],
                                                m1[:, 0:1], None, ALU.mult)),
           waits=[(sLN2, None)], incs=[(sLN2, 1)])
        st("vector", (lambda v: v.scalar_tensor_tensor(
            vr[:, 0:1], s2[:, 0:1], 1.0 / D, msq[:, 0:1], ALU.mult,
            ALU.subtract)),
           waits=[(sSQ2, None), (sLN2, None)], incs=[(sLN2, 1)])
        st("scalar", (lambda s: s.activation(rstd[:, 0:1], vr[:, 0:1], AF.Sqrt,
                                             bias=C["epsb"][:, 0:1], scale=1.0)),
           waits=[(sLN2, None)], incs=[(sSQ2, 1)])
        st("vector", (lambda v: v.reciprocal(rstd[:, 0:1], rstd[:, 0:1])),
           waits=[(sSQ2, None)], incs=[(sLN2, 1)])
        st("vector", (lambda v, gidx=gidx, pp=pp: v.tensor_scalar(
            outt[:, pp * 1024:(pp + 1) * 1024],
            x1[:, gidx * 1024:(gidx + 1) * 1024], m1[:, 0:1], rstd[:, 0:1],
            ALU.subtract, ALU.mult)),
           waits=[(sLN2, None)], incs=[(sLN2, 1)])
        b_, row = gidx // 16, (gidx % 16) * 128
        st("sync", (lambda s, b_=b_, row=row, pp=pp: s.dma_start(
            out=y[b_, row:row + 128, :], in_=outt[:, pp * 1024:(pp + 1) * 1024])),
           waits=[(sLN2, None)], incs=[(sOut, 16)])
    st("sync", None, waits=[(sOut, 32 * 16)])
    st("gpsimd", None, waits=[(sOut, 32 * 16)])
    st("vector", None, waits=[(sOut, 32 * 16)])
    return hd




def emit_ffn_sparse(p, din, hd):
    st = p.step
    nc = p.nc
    C = hd["consts"]
    x1 = hd["x1"]
    wvals = hd["wvals"]
    dsti = hd["dsti"]
    sRt, rt_done = hd["routing"]["sRt"], hd["routing"]["rt"]
    sBi, bi_done = hd["routing"]["sBi"], hd["routing"]["bi"]
    sPG = hd["routing"]["sPG"]
    sDs, ds_done = hd["routing"]["sDs"], hd["routing"]["ds"]
    CP = 768
    MT = CP // 128

    sCast = p.sem("sCast")
    sZt = p.sem("sZt")
    sZe = p.sem("sZe")
    sSc = p.sem("sSc")
    sXg = p.sem("sXg")
    sTr3 = p.sem("fTr")
    sTrE3 = p.sem("fTrE")
    sW1 = p.sem("fW1")
    sW2 = p.sem("fW2")
    sS1 = p.sem("fS1")
    sGl = p.sem("fGl")
    sS2 = p.sem("fS2")
    sHe = p.sem("fHe")
    sHs = p.sem("fHs")
    sHg = p.sem("fHg")
    sLN2 = p.sem("fLN2")
    sSQ2 = p.sem("fSQ2")
    sOut = p.sem("fOut")

    xgd = nc.dram_tensor("xgd", [E * CP, D], BF16, kind="Internal").ap()
    hdd = nc.dram_tensor("hdd", [E * CP, D], BF16, kind="Internal").ap()

    dstint = p.sbuf("dstint", [128, 32], mybir.dt.int32)
    zt = p.sbuf("zt", [128, 1024], BF16)
    xgs = p.sbuf("xgs", [128, 2 * 1024], BF16)
    xgT = p.sbuf("xgT", [128, 8 * CP], BF16)
    w1s = p.sbuf("w1s", [128, 2 * 2048], BF16)
    w2s = p.sbuf("w2s", [128, 2 * 8192], BF16)
    geluT = p.sbuf("geluT", [128, 16 * CP], BF16)
    hbuf = p.sbuf("hbuf", [128, 2 * 1024], BF16)
    hgs = p.sbuf("hgs", [128, 4 * 1024], BF16)
    z2b = p.sbuf("z2b", [128, 1024], F32)
    outt = p.sbuf("outt", [128, 2 * 1024], F32)
    s1 = p.sbuf("f_s1", [128, 1], F32)
    s2 = p.sbuf("f_s2", [128, 1], F32)
    m1 = p.sbuf("f_m1", [128, 1], F32)
    msq = p.sbuf("f_msq", [128, 1], F32)
    vr = p.sbuf("f_vr", [128, 1], F32)
    rstd = p.sbuf("f_rstd", [128, 1], F32)

    pF = p.psum("pF", [128, 2048], F32)
    pTr3 = p.psum("fpTr", [128, 2048], BF16)

    y = nc.dram_tensor("y", [B, S, D], F32, kind="ExternalOutput").ap()

    st("vector", (lambda v: v.memset(zt[:], 0.0)), incs=[(sZt, 1)])
    st("vector", (lambda v: v.tensor_copy(dstint[:], dsti[:])),
       waits=[(sDs, ds_done)], incs=[(sCast, 1)])
    if hd.get("dbg_sp"):
        st("sync", (lambda s: s.dma_start(out=hd["dbg_sp"][0], in_=dsti[:])),
           waits=[(sDs, ds_done)], incs=[(hd["sDbg"], 16)])

    for r in range(E * CP // 128):
        st("sync", (lambda s, r=r: s.dma_start(
            out=xgd[r * 128:(r + 1) * 128, :], in_=zt[:])),
           waits=[(sZt, 1)], incs=[(sZe, 16)])
    ze_done = sZe.count

    for i in range(32):
        st("gpsimd", (lambda g, i=i: g.indirect_dma_start(
            xgd[0:128, :], IndirectOffsetOnAxis(dstint[:, i:i + 1], 0),
            x1[:, i * 1024:(i + 1) * 1024], None)),
           waits=[(sCast, 1), (sZe, ze_done)], incs=[(sSc, 16)])
    sc_done = sSc.count
    if hd.get("dbg_sp"):
        st("sync", (lambda s: s.dma_start(out=hd["dbg_sp"][1],
                                          in_=xgd[0:256, :])),
           waits=[(sSc, sc_done)], incs=[(hd["sDbg"], 16)])

    w1ld = [0]
    w2ld = [0]
    for e in range(E):
        for m in range(MT):
            slot = (e * MT + m) % 2
            w = [(sSc, sc_done)]
            if e * MT + m >= 2:
                w.append((sTr3, sTr3.count - 8))
            st("sync", (lambda s, e=e, m=m, slot=slot: s.dma_start(
                out=xgs[:, slot * 1024:(slot + 1) * 1024],
                in_=xgd[e * CP + m * 128: e * CP + (m + 1) * 128, :])),
               waits=w, incs=[(sXg, 16)])
            for c in range(8):
                ppt = sTr3.count % 2
                w = [(sXg, sXg.count)]
                if sTr3.count >= 2:
                    w.append((sTrE3, sTrE3.count - 1))
                if e > 0 and m == 0 and c == 0:
                    w.append((sS1, sS1.count))
                st("tensor", (lambda t, slot=slot, c=c, ppt=ppt: t.transpose(
                    pTr3[:, ppt * 1024: ppt * 1024 + 128],
                    xgs[:, slot * 1024 + c * 128: slot * 1024 + (c + 1) * 128],
                    C["idn_bf"][:])),
                   waits=w, incs=[(sTr3, 1)])
                st("vector", (lambda v, m=m, c=c, ppt=ppt: v.tensor_copy(
                    xgT[:, c * CP + m * 128: c * CP + (m + 1) * 128],
                    pTr3[:, ppt * 1024: ppt * 1024 + 128])),
                   waits=[(sTr3, None)], incs=[(sTrE3, 1)])
        trg_done = sTrE3.count

        for f2cg in range(8):
            slot = w1ld[0] % 2
            w = [(sBi, bi_done), (sPG, 16)]
            if w1ld[0] >= 2:
                w.append((sS1, sS1.count - 1))
            st("gpsimd", (lambda g, e=e, f2cg=f2cg, slot=slot: g.dma_start(
                out=w1s[:, slot * 2048:(slot + 1) * 2048].rearrange(
                    "p (k c) -> p k c", k=8),
                in_=din["w1r"][e, f2cg])),
               waits=w, incs=[(sW1, 16)])
            w1ld[0] += 1
            ldmark = sW1.count
            for k in range(8):
                for j in range(2):
                    for sub in range(2):
                        bk = j * 2 + sub
                        w = []
                        if k == 0 and j == 0 and sub == 0:
                            w = [(sW1, ldmark), (sTrE3, trg_done),
                                 (sHe, sHe.count), (sGl, sGl.count)]
                        st("tensor", (lambda t, slot=slot, j=j, sub=sub, bk=bk,
                                      k=k: t.matmul(
                            pF[:, bk * 512: bk * 512 + 384],
                            w1s[:, slot * 2048 + k * 256 + j * 128:
                                slot * 2048 + k * 256 + (j + 1) * 128],
                            xgT[:, k * CP + sub * 384: k * CP + (sub + 1) * 384],
                            start=(k == 0), stop=(k == 7))),
                           waits=w,
                           incs=[(sS1, 1)] if (k == 7 and bk == 3) else [])
            s1_mark = sS1.count
            for j in range(2):
                f2c = f2cg * 2 + j
                for sub in range(2):
                    bk = j * 2 + sub
                    st("scalar", (lambda s, f2c=f2c, sub=sub, bk=bk:
                                  s.activation(
                        geluT[:, f2c * CP + sub * 384: f2c * CP + (sub + 1) * 384],
                        pF[:, bk * 512: bk * 512 + 384], AF.Gelu)),
                       waits=[(sS1, s1_mark), (sS2, sS2.count)],
                       incs=[(sGl, 1)])
        gl_mark = sGl.count

        for half in range(2):
            slot = w2ld[0] % 2
            w = []
            if w2ld[0] >= 2:
                w.append((sS2, sS2.count - MT))
            st("gpsimd", (lambda g, e=e, half=half, slot=slot: g.dma_start(
                out=w2s[:, slot * 8192:(slot + 1) * 8192].rearrange(
                    "p (c f) -> p c f", c=16),
                in_=din["w2r"][e, half])),
               waits=w, incs=[(sW2, 16)])
            w2ld[0] += 1
            ld2mark = sW2.count
            for m in range(MT):
                pp = m % 2
                for f2c in range(16):
                    w = []
                    if f2c == 0:
                        w = [(sW2, ld2mark), (sGl, gl_mark)]
                        if not (m < 2 and half == 0 and e == 0):
                            w.append((sHe, sHe.count - 1))
                    st("tensor", (lambda t, pp=pp, f2c=f2c, m=m, slot=slot:
                                  t.matmul(
                        pF[:, pp * 512: (pp + 1) * 512],
                        geluT[:, f2c * CP + m * 128: f2c * CP + (m + 1) * 128],
                        w2s[:, slot * 8192 + f2c * 512:
                            slot * 8192 + (f2c + 1) * 512],
                        start=(f2c == 0), stop=(f2c == 15))),
                       waits=w, incs=[(sS2, 1)] if f2c == 15 else [])
                s2m = sS2.count
                hslot = (sHe.count) % 2
                w = [(sS2, s2m)]
                if sHe.count >= 2:
                    w.append((sHs, (sHe.count - 1) * 16))
                st("vector", (lambda v, pp=pp, hslot=hslot, half=half:
                              v.tensor_copy(
                    hbuf[:, hslot * 1024 + half * 512:
                         hslot * 1024 + half * 512 + 512],
                    pF[:, pp * 512: (pp + 1) * 512])),
                   waits=w, incs=[(sHe, 1)])
                st("sync", (lambda s, e=e, m=m, hslot=hslot, half=half:
                            s.dma_start(
                    out=hdd[e * CP + m * 128: e * CP + (m + 1) * 128,
                            half * 512: half * 512 + 512],
                    in_=hbuf[:, hslot * 1024 + half * 512:
                             hslot * 1024 + half * 512 + 512])),
                   waits=[(sHe, None)], incs=[(sHs, 16)])
    ffn_done = sHs.count
    if hd.get("dbg_sp"):
        st("sync", (lambda s: s.dma_start(out=hd["dbg_sp"][2],
                                          in_=hdd[0:256, :])),
           waits=[(sHs, ffn_done)], incs=[(hd["sDbg"], 16)])

    for gidx in range(32):
        gslot = gidx % 4
        w = [(sHs, ffn_done), (sCast, 1)]
        if gidx >= 4:
            w.append((sLN2, (gidx - 3) * 7))
        st("gpsimd", (lambda g, gidx=gidx, gslot=gslot: g.indirect_dma_start(
            hgs[:, gslot * 1024:(gslot + 1) * 1024], None,
            hdd[0:128, :],
            IndirectOffsetOnAxis(dstint[:, gidx:gidx + 1], 0))),
           waits=w, incs=[(sHg, 16)])
        st("vector", (lambda v, gidx=gidx, gslot=gslot: v.scalar_tensor_tensor(
            z2b[:], hgs[:, gslot * 1024:(gslot + 1) * 1024],
            wvals[:, gidx:gidx + 1], x1[:, gidx * 1024:(gidx + 1) * 1024],
            ALU.mult, ALU.add)),
           waits=[(sHg, 16 * (gidx + 1)), (sRt, rt_done), (sSQ2, sSQ2.count)],
           incs=[(sLN2, 1)])
        st("vector", (lambda v: v.reduce_sum(s1[:, 0:1], z2b[:], AX.X)),
           waits=[(sLN2, None)], incs=[(sLN2, 1)])
        st("scalar", (lambda s: s.activation(zt[:], z2b[:], AF.Square,
                                             accum_out=s2[:, 0:1])),
           waits=[(sLN2, sLN2.count), (sSQ2, sSQ2.count)], incs=[(sSQ2, 1)])
        st("vector", (lambda v: v.tensor_scalar_mul(m1[:, 0:1], s1[:, 0:1],
                                                    1.0 / D)),
           waits=[(sLN2, None)], incs=[(sLN2, 1)])
        st("vector", (lambda v: v.tensor_scalar(msq[:, 0:1], m1[:, 0:1],
                                                m1[:, 0:1], None, ALU.mult)),
           waits=[(sLN2, None)], incs=[(sLN2, 1)])
        st("vector", (lambda v: v.scalar_tensor_tensor(
            vr[:, 0:1], s2[:, 0:1], 1.0 / D, msq[:, 0:1], ALU.mult,
            ALU.subtract)),
           waits=[(sSQ2, None), (sLN2, None)], incs=[(sLN2, 1)])
        st("scalar", (lambda s: s.activation(rstd[:, 0:1], vr[:, 0:1], AF.Sqrt,
                                             bias=C["epsb"][:, 0:1], scale=1.0)),
           waits=[(sLN2, None)], incs=[(sSQ2, 1)])
        st("vector", (lambda v: v.reciprocal(rstd[:, 0:1], rstd[:, 0:1])),
           waits=[(sSQ2, None)], incs=[(sLN2, 1)])
        pp = gidx % 2
        st("vector", (lambda v, pp=pp: v.tensor_scalar(
            outt[:, pp * 1024:(pp + 1) * 1024], z2b[:], m1[:, 0:1],
            rstd[:, 0:1], ALU.subtract, ALU.mult)),
           waits=[(sLN2, None), (sOut, max(0, (gidx - 1) * 16))],
           incs=[(sLN2, 1)])
        b_, row = gidx // 16, (gidx % 16) * 128
        st("sync", (lambda s, b_=b_, row=row, pp=pp: s.dma_start(
            out=y[b_, row:row + 128, :], in_=outt[:, pp * 1024:(pp + 1) * 1024])),
           waits=[(sLN2, None)], incs=[(sOut, 16)])
    st("sync", None, waits=[(sOut, 32 * 16)])
    st("gpsimd", None, waits=[(sOut, 32 * 16)])
    st("vector", None, waits=[(sOut, 32 * 16)])
    return hd

def build_full(fused=True, debug=False):
    nc = bass.Bass("TRN2", target_bir_lowering=False, debug=False)
    p = Prog(nc)
    din = declare_inputs(nc)
    hd = {}
    if debug:
        hd["dbg_rt"] = (
            nc.dram_tensor("d_th", [128, 8], F32, kind="ExternalOutput").ap(),
            nc.dram_tensor("d_ga", [128, 32 * 8], F32, kind="ExternalOutput").ap(),
            nc.dram_tensor("d_pg", [64, T], F32, kind="ExternalOutput").ap(),
        )
        hd["dbg_rt2"] = (
            nc.dram_tensor("d_gT", [8, T], F32, kind="ExternalOutput").ap(),
            nc.dram_tensor("d_poss", [8, T], F32, kind="ExternalOutput").ap(),
        )
        hd["dbg_sp"] = (
            nc.dram_tensor("d_dst", [128, 32], F32, kind="ExternalOutput").ap(),
            nc.dram_tensor("d_xg", [256, 1024], BF16, kind="ExternalOutput").ap(),
            nc.dram_tensor("d_hd", [256, 1024], BF16, kind="ExternalOutput").ap(),
        )
        hd["dbg_ffn"] = (
            nc.dram_tensor("d_gel", [128, 16 * 1024], BF16,
                           kind="ExternalOutput").ap(),
            nc.dram_tensor("d_z2", [128, 32 * 1024], BF16,
                           kind="ExternalOutput").ap(),
        )
    hd["consts"] = dict(
        idn_bf=p.sbuf("c_idnbf", [128, 128], BF16),
        bq=p.sbuf("c_bq", [128, 24], F32),
        ones_f32=p.sbuf("c_ones", [1, 128], F32),
        bout_bc=p.sbuf("c_boutbc", [128, 1024], BF16),
        expb=p.sbuf("c_expb", [128, 1], F32),
        epsb=p.sbuf("c_epsb", [128, 1], F32),
        ones32=p.sbuf("c_ones32", [128, 64], BF16),
        ones1b=p.sbuf("c_ones1b", [1, 128], BF16),
    )
    hd["x1"] = p.sbuf("x1", [128, 32 * 1024], BF16)
    hd["boutstage"] = hd["x1"]

    if debug:
        hd["sDbg"] = p.sem("sDbg")
    emit_attention(p, din, hd)

    p.open_phase()
    hd["probs"] = p.sbuf("probs", [128, 32 * 8], F32)
    hd["logits"] = p.sbuf("logits", [128, 32 * 8], F32)
    hd["gates"] = p.sbuf("gates", [128, 32 * 8], F32)
    hd["th8"] = p.sbuf("th8", [128, 8], F32)
    hd["idn_f32"] = p.sbuf("idn_f32", [128, 128], F32)
    hd["wg_sb"] = p.sbuf("wg_sb", [128, 64], BF16)

    hd["wvals"] = p.sbuf("wvals", [128, 32], F32)
    hd["dsti"] = p.sbuf("dsti", [128, 32], F32)
    p.open_phase()
    hd["ccsb"] = p.sbuf("ccsb", [8, T], F32)
    p.open_phase()
    emit_gating(p, din, hd)
    p.close_phase()
    emit_routing(p, din, hd, fused=fused)
    p.close_phase()

    emit_ffn_sparse(p, din, hd)
    p.close_phase()

    p.build_block()
    p.stack.close()
    return nc


def host_inmaps(inputs):
    Smat = np.zeros((64, 8), np.float32)
    for q in range(64):
        Smat[q, q % 8] = 1.0
    base = {
        "wqkv_r": np.ascontiguousarray(
            np.asarray(inputs["L0_attn_W_qkv"], np.float32)
            .reshape(8, 128, 24, 128).transpose(2, 1, 0, 3)),
        "bqkv_pc": np.ascontiguousarray(
            np.asarray(inputs["L0_attn_b_qkv"], np.float32).reshape(24, 128).T),
        "wout_r": np.ascontiguousarray(
            np.asarray(inputs["L0_attn_W_out"], np.float32)
            .reshape(8, 128, 1024).transpose(1, 0, 2)),
        "bout": np.ascontiguousarray(np.asarray(inputs["L0_attn_b_out"], np.float32)),
        "wg_r": np.ascontiguousarray(
            np.asarray(inputs["L0_ffn_Wg"], np.float32)
            .reshape(8, 128, 8).transpose(1, 0, 2)),
        "w1r": np.ascontiguousarray(
            np.asarray(inputs["L0_ffn_W1"], np.float32)
            .reshape(E, 8, 128, 8, 256).transpose(0, 3, 2, 1, 4)),
        "w2r": np.ascontiguousarray(
            np.asarray(inputs["L0_ffn_W2"], np.float32)
            .reshape(E, 16, 128, 2, 512).transpose(0, 3, 2, 1, 4)),
        "idn_bf": np.eye(128).astype(BFD),
        "idn_f32": np.eye(128, dtype=np.float32),
        "ones_f32": np.ones((1, 128), np.float32),
        "smat": Smat,
        "s2mat": np.ascontiguousarray(Smat.T),
        "e768": np.tile(np.arange(8, dtype=np.float32) * 768.0, 32).reshape(1, 256),
    }
    x = np.asarray(inputs["L0_attn_x"], np.float32)
    maps = []
    for c in range(NCORES):
        m = dict(base)
        m["x"] = np.ascontiguousarray(x[c * B:(c + 1) * B])
        maps.append(m)
    return maps


_NC_CACHE = {}


def kernel(**inputs):
    if "nc" not in _NC_CACHE:
        _NC_CACHE["nc"] = build_full(fused=True)
    nc = _NC_CACHE["nc"]
    maps = host_inmaps(inputs)
    res = run_bass_kernel_spmd(nc, maps, core_ids=list(range(NCORES)))
    return np.concatenate([np.asarray(r["y"]) for r in res.results], axis=0)
